# version 10
# baseline (speedup 1.0000x reference)
import math
from contextlib import ExitStack

import numpy as np

import concourse.bass as bass
import concourse.mybir as mybir
from concourse.bass_utils import run_bass_kernel_spmd

F32 = mybir.dt.float32
BF16 = mybir.dt.bfloat16
AF = mybir.ActivationFunctionType
ALU = mybir.AluOpType
AX = mybir.AxisListType

NEG = -1.0e30
NCORE = 8
TOWN = 2048
HALO = 128
TH = TOWN + HALO
NXT = TH // 128
D, DS, FF, LB = 1024, 256, 4096, 64
NKD = D // 128
NKF = FF // 128


class Ctr:
    __slots__ = ("h", "v")

    def __init__(self, nc, name):
        self.h = nc.alloc_semaphore(name)
        self.v = 0

    def inc(self, instr, n=1):
        instr.then_inc(self.h, n)
        self.v += n
        return self.v


def build_program():
    nc = bass.Bass("TRN2", target_bir_lowering=False,
                   detect_race_conditions=False)

    def din(name, shape, dt=F32):
        return nc.dram_tensor(name, shape, dt, kind="ExternalInput").ap()

    xin = din("xin", [TH, D])
    wi_d = din("wi", [D, DS])
    wq_d = din("wq", [DS, DS])
    wk_d = din("wk", [DS, DS])
    wv_d = din("wv", [DS, DS])
    wp_d = din("wp", [DS, DS])
    w2a_d = din("w2a", [DS, DS])
    w1a_d = din("w1a", [D, DS])
    wh_d = din("wh", [DS, 3])
    wo_d = din("wo", [DS, D])
    w1f_d = din("w1f", [D, FF])
    w2f_d = din("w2f", [FF, D])
    bias2_d = {}
    for nm in ("bi_t", "bq_t", "bk_t", "bv_t", "bp_t", "b2a_t", "b1a_t", "cb_t"):
        bias2_d[nm] = din(nm, [128, 2])
    b1f_d = din("b1f_t", [128, 32])
    cw_d = din("cw_t", [128, 10])
    bh_d = din("bh_r", [1, 3])
    bo_d = din("bo_r", [1, D])
    b2f_d = din("b2f_r", [1, D])
    ident_d = din("ident", [128, 128])
    bandm_d = din("bandm", [128, 192])
    blk0m_d = din("blk0m", [128, 192])
    hscal_d = din("halo_scal", [128, 1])
    t0scal_d = din("tok0_scal", [128, 1])

    import os as _os
    KDUMP = bool(_os.environ.get("KDUMP"))
    xout = nc.dram_tensor("xout", [TOWN, D], F32, kind="ExternalOutput").ap()
    if _os.environ.get("KFFN1"):
        d_st = nc.dram_tensor("d_st", [128, 32], F32, kind="ExternalOutput").ap()
        d_w1tl8 = nc.dram_tensor("d_w1tl8", [NKD, 128, FF], BF16, kind="ExternalOutput").ap()
        d_f1all = None
    if _os.environ.get("KF1"):
        d_f1all = nc.dram_tensor("d_f1all", [9 * NKF, 128, 256], F32, kind="ExternalOutput").ap()
    if KDUMP:
        d_x1 = nc.dram_tensor("d_x1", [TH, D], F32, kind="ExternalOutput").ap()
        d_h0 = nc.dram_tensor("d_h0", [2, 128, TH], F32, kind="ExternalOutput").ap()
        d_h1 = nc.dram_tensor("d_h1", [2, 128, 2112], F32, kind="ExternalOutput").ap()
        d_h2 = nc.dram_tensor("d_h2", [2, 128, TOWN], F32, kind="ExternalOutput").ap()
        d_x2 = nc.dram_tensor("d_x2", [TH, D], F32, kind="ExternalOutput").ap()

    scr_kind = {"kind": "ExternalOutput"} if KDUMP else {}
    w1scr = nc.dram_tensor("w1scr", [D, FF], BF16, **scr_kind).ap()
    w2scr = nc.dram_tensor("w2scr", [FF, D], BF16, **scr_kind).ap()
    if KDUMP:
        d_tern = nc.dram_tensor("d_tern", [128, 8], F32, kind="ExternalOutput").ap()
        d_w1tl = nc.dram_tensor("d_w1tl", [128, FF], BF16, kind="ExternalOutput").ap()
        d_w2T = nc.dram_tensor("d_w2T", [128, D], BF16, kind="ExternalOutput").ap()
        d_yhi = nc.dram_tensor("d_yhi", [128, 256], BF16, kind="ExternalOutput").ap()
        d_ylo = nc.dram_tensor("d_ylo", [128, 256], BF16, kind="ExternalOutput").ap()
        d_f1 = nc.dram_tensor("d_f1", [128, 256], F32, kind="ExternalOutput").ap()
        d_bst = nc.dram_tensor("d_bst", [128, 12], F32, kind="ExternalOutput").ap()
        d_stA = nc.dram_tensor("d_stA", [128, 34], F32, kind="ExternalOutput").ap()
        d_stC = nc.dram_tensor("d_stC", [128, 34], F32, kind="ExternalOutput").ap()

    def sal(name, shape, dt=F32):
        return nc.alloc_sbuf_tensor(name, shape, dt)

    x = [sal(f"x{j}", [128, D]) for j in range(NXT)]

    wq_sb = [sal(f"wq{k}", [128, DS]) for k in range(2)]
    wk_sb = [sal(f"wk{k}", [128, DS]) for k in range(2)]
    wv_sb = [sal(f"wv{k}", [128, DS]) for k in range(2)]
    wp_sb = [sal(f"wp{k}", [128, DS]) for k in range(2)]
    w2a_sb = [sal(f"w2a{k}", [128, DS]) for k in range(2)]

    wh_sb = [sal(f"wh{k}", [128, 3]) for k in range(2)]
    wo_sb = [sal(f"wo{k}", [128, D]) for k in range(2)]
    bias2 = {nm: sal(nm + "_s", [128, 2]) for nm in
             ("bi_t", "bq_t", "bk_t", "bv_t", "bp_t", "b2a_t", "b1a_t", "cb_t")}
    b1f_sb = sal("b1f_s", [128, 32])
    cw_sb = sal("cw_s", [128, 10])
    bh_sb = sal("bh_s", [1, 3])
    bo_sb = sal("bo_s", [1, D])
    b2f_sb = sal("b2f_s", [1, D])
    ident = sal("ident_s", [128, 128])
    bandm = sal("bandm_s", [128, 192])
    blk0m = sal("blk0m_s", [128, 192])
    hscal = sal("hscal_s", [128, 1])
    t0scal = sal("t0scal_s", [128, 1])
    ones_r = sal("ones_r", [1, 128])
    zrow16 = sal("zrow16", [1, 8], BF16)
    ones_c = sal("ones_c", [128, 1])
    eps_t = sal("eps_t", [128, 1])
    sc16 = sal("sc16", [128, 1])

    nmu1 = sal("nmu1", [128, NXT]); rsd1 = sal("rsd1", [128, NXT])
    nmu2 = sal("nmu2", [128, NXT]); rsd2 = sal("rsd2", [128, NXT])
    nmu3 = sal("nmu3", [128, 16]);  rsd3 = sal("rsd3", [128, 16])
    a2t = sal("a2t", [128, NXT]); b2ct = sal("b2ct", [128, NXT])
    var_all = sal("var_all", [128, NXT])
    sd_all = sal("sd_all", [128, NXT])
    bst = sal("bst", [128, 12])
    if KDUMP:
        bst12 = sal("bst12", [128, 12])
        d_stA_sb = sal("d_stA_sb", [128, 34])
    bagg = sal("bagg", [128, 2])
    rtmp = sal("rtmp", [128, 32])
    racc = sal("racc", [128, 1])
    tot1 = sal("tot1", [1, 1])
    half1 = sal("half1", [128, 1]); nhalf1 = sal("nhalf1", [128, 1])
    half2 = sal("half2", [128, 1]); nhalf2 = sal("nhalf2", [128, 1])

    p = [nc.alloc_psum_tensor(f"p{i}", [128, 512], F32) for i in range(8)]

    sx = Ctr(nc, "sx")
    swa = Ctr(nc, "swa")
    swr = Ctr(nc, "swr")
    sscr = Ctr(nc, "sscr")
    swt = Ctr(nc, "swt")
    sout = Ctr(nc, "sout")
    cv = Ctr(nc, "cv")
    cs = Ctr(nc, "cs")
    ct = Ctr(nc, "ct")
    cg = Ctr(nc, "cg")

    with nc.Block() as _blk:
        V, S, T, G, SY = nc.vector, nc.scalar, nc.tensor, nc.gpsimd, nc.sync


        def tg(ap, n=1):
            return ct.inc(T.matmul(ap, zrow16[0:1, 0:1], zrow16[0:1, 0:1],
                                   start=False, stop=True,
                                   skip_group_check=True), n)

        def barrier(engines):
            for e in engines:
                for c in (cv, cs, ct, cg):
                    e.wait_ge(c.h, c.v)

        for k in range(2):
            r = slice(128 * k, 128 * (k + 1))
            swa.inc(G.dma_start(out=wq_sb[k][:], in_=wq_d[r, :]), 16)
            swa.inc(G.dma_start(out=wk_sb[k][:], in_=wk_d[r, :]), 16)
            swa.inc(G.dma_start(out=wv_sb[k][:], in_=wv_d[r, :]), 16)
            swa.inc(G.dma_start(out=wp_sb[k][:], in_=wp_d[r, :]), 16)
            swa.inc(G.dma_start(out=w2a_sb[k][:], in_=w2a_d[r, :]), 16)
            swa.inc(G.dma_start(out=wh_sb[k][:], in_=wh_d[r, :]), 16)
            swa.inc(G.dma_start(out=wo_sb[k][:], in_=wo_d[r, :]), 16)
        for nm, dd in bias2_d.items():
            swa.inc(G.dma_start(out=bias2[nm][:], in_=dd), 16)
        for dst, src in ((b1f_sb, b1f_d), (cw_sb, cw_d), (bh_sb, bh_d),
                         (bo_sb, bo_d), (b2f_sb, b2f_d), (ident, ident_d),
                         (bandm, bandm_d), (blk0m, blk0m_d), (hscal, hscal_d),
                         (t0scal, t0scal_d)):
            swa.inc(G.dma_start(out=dst[:], in_=src), 16)
        swa_done = swa.v

        V.memset(ones_r[:], 1.0)
        V.memset(zrow16[:], 0.0)
        V.memset(ones_c[:], 1.0)
        V.memset(eps_t[:], 1e-5)
        cv.inc(V.memset(sc16[:], 1.0 / 16.0))
        consts_ready = cv.v
        T.wait_ge(cv.h, consts_ready)
        T.matmul(p[5][0:1, 0:1], zrow16[0:1, 0:1], zrow16[0:1, 0:1],
                 start=True, stop=True, skip_group_check=True)

        def emit_stats_tile(jj, j, nmu_t):
            g0 = cv.inc(V.bn_stats(bst[:, 0:6], x[j][:, 0:512]))
            V.wait_ge(cv.h, g0)
            g1 = cv.inc(V.bn_stats(bst[:, 6:12], x[j][:, 512:1024]))
            V.wait_ge(cv.h, g1)
            g2 = cv.inc(V.bn_aggr(bagg[:], bst[:]))
            V.wait_ge(cv.h, g2)
            V.tensor_scalar_mul(nmu_t[:, jj:jj + 1], bagg[:, 0:1], -1.0)
            return V.tensor_copy(out=var_all[:, jj:jj + 1], in_=bagg[:, 1:2])

        def emit_stats_fin(n, rsd_t, last_cp):
            g3 = cv.inc(last_cp)
            S.wait_ge(cv.h, g3)
            s1 = cs.inc(S.activation(out=sd_all[:, 0:n], in_=var_all[:, 0:n],
                                     func=AF.Sqrt, bias=eps_t[:, 0:1], scale=1.0))
            V.wait_ge(cs.h, s1)
            g4 = cv.inc(V.reciprocal(rsd_t[:, 0:n], sd_all[:, 0:n]))
            V.wait_ge(cv.h, g4)
            return cv.v

        def emit_stats(tiles, nmu_t, rsd_t, wait_fn):
            for jj, j in enumerate(tiles):
                wait_fn(jj, j)
                last_cp = emit_stats_tile(jj, j, nmu_t)
            return emit_stats_fin(len(tiles), rsd_t, last_cp)

        def wait_x(jj, j):
            V.wait_ge(sx.h, 32 * NXT)

        with ExitStack() as es:
            w1raw = [es.enter_context(nc.sbuf_tensor(f"w1raw{i}", [128, 1024], F32))
                     for i in range(7)]
            w2raw = [es.enter_context(nc.sbuf_tensor(f"w2raw{i}", [128, 1024], F32))
                     for i in range(7)]
            tmpa = [es.enter_context(nc.sbuf_tensor(f"tmpa{i}", [128, 1024], F32))
                    for i in range(2)]
            tmpb1 = [es.enter_context(nc.sbuf_tensor(f"tmpb1_{i}", [128, 1024], F32))
                     for i in range(2)]
            tmpb2 = [es.enter_context(nc.sbuf_tensor(f"tmpb2_{i}", [128, 1024], F32))
                     for i in range(2)]
            t16a = [es.enter_context(nc.sbuf_tensor(f"t16a{i}", [128, 1024], BF16))
                    for i in range(3)]
            t16b = [es.enter_context(nc.sbuf_tensor(f"t16b{i}", [128, 1024], BF16))
                    for i in range(3)]
            rtmp2 = es.enter_context(nc.sbuf_tensor("rtmp2", [128, 32], F32))
            racc2 = es.enter_context(nc.sbuf_tensor("racc2", [128, 1], F32))
            tot2 = es.enter_context(nc.sbuf_tensor("tot2", [1, 1], F32))

            w1slices = [(r, s) for r in range(NKD) for s in range(4)]
            nW = len(w1slices)
            PF = 6
            w1A_dv = []
            w2A_dv = []
            w1A_red_done = []
            w2A_red_done = []

            def issue_w1(i, dvl):
                r, s = w1slices[i]
                for q4 in range(4):
                    cq = slice(256 * q4, 256 * (q4 + 1))
                    last = swr.inc(SY.dma_start(
                        out=w1raw[i % 7][:, cq],
                        in_=w1f_d[128 * r:128 * (r + 1),
                                  1024 * s + 256 * q4:1024 * s + 256 * (q4 + 1)]),
                        16)
                dvl.append(last)

            def issue_w2(i, dvl):
                for q4 in range(4):
                    cq = slice(256 * q4, 256 * (q4 + 1))
                    last = swr.inc(SY.dma_start(
                        out=w2raw[i % 7][:, cq],
                        in_=w2f_d[128 * i:128 * (i + 1), cq]), 16)
                dvl.append(last)

            for i in range(PF):
                issue_w1(i, w1A_dv)
                issue_w2(i, w2A_dv)
            for j in range(nW):
                if j < NXT:
                    r_ = slice(128 * j, 128 * (j + 1))
                    sx.inc(SY.dma_start(out=x[j][:, 0:512],
                                        in_=xin[r_, 0:512]), 16)
                    sx.inc(SY.dma_start(out=x[j][:, 512:1024],
                                        in_=xin[r_, 512:1024]), 16)
                i = j + PF
                if i < nW:
                    if i >= 7:
                        SY.wait_ge(cs.h, w1A_red_done[i - 7])
                        SY.wait_ge(cv.h, w2A_red_done[i - 7])
                    issue_w1(i, w1A_dv)
                    issue_w2(i, w2A_dv)
                S.wait_ge(swr.h, w1A_dv[min(j + 3, nW - 1)])
                if j == nW - 1:
                    S.wait_ge(sx.h, 32)
                    d0 = cs.inc(S.activation(out=tmpa[0][:], in_=x[0][:],
                                             func=AF.Abs, bias=0.0, scale=1.0))
                    S.wait_ge(cs.h, d0)
                w1A_red_done.append(cs.inc(S.activation(
                    out=tmpa[j % 2][:], in_=w1raw[j % 7][:], func=AF.Abs,
                    bias=0.0, scale=1.0, accum_out=rtmp[:, j:j + 1])))
                S.wait_ge(cs.h, w1A_red_done[-1])
                V.wait_ge(swr.h, w2A_dv[min(j + 3, nW - 1)])
                if j == nW - 1:
                    V.wait_ge(sx.h, 32)
                    d0 = cv.inc(V.tensor_copy(out=tmpb1[0][:], in_=x[0][:]))
                    V.wait_ge(cv.h, d0)
                w2A_red_done.append(cv.inc(V.reduce_sum(
                    rtmp2[:, j:j + 1], w2raw[j % 7][:], axis=AX.X,
                    apply_absolute_value=True)))
                V.wait_ge(cv.h, w2A_red_done[-1])
            V.wait_ge(cs.h, w1A_red_done[-1])
            g = cv.inc(V.reduce_sum(racc[:], rtmp[:], axis=AX.X))
            V.wait_ge(cv.h, g)
            w1A_red = cv.v
            g = cv.inc(V.reduce_sum(racc2[:], rtmp2[:], axis=AX.X))
            V.wait_ge(cv.h, g)
            w2A_red = cv.v

            T.wait_ge(cv.h, w1A_red)
            T.wait_ge(cv.h, w2A_red)
            T.matmul(p[0][0:1, 0:1], racc[:], ones_c[:], start=True, stop=True)
            T.matmul(p[1][0:1, 0:1], racc2[:], ones_c[:], start=True, stop=True)
            t1 = tg(p[5][0:1, 0:1])
            V.wait_ge(ct.h, t1)
            V.tensor_scalar(out=tot1[:], in0=p[0][0:1, 0:1],
                            scalar1=1.0 / (FF * D), scalar2=1e-8,
                            op0=ALU.mult, op1=ALU.max)
            g = cv.inc(V.tensor_scalar(out=tot2[:], in0=p[1][0:1, 0:1],
                                       scalar1=1.0 / (FF * D), scalar2=1e-8,
                                       op0=ALU.mult, op1=ALU.max))
            T.wait_ge(cv.h, g)
            T.matmul(p[0][0:128, 0:1], ones_r[:], tot1[:], start=True, stop=True)
            T.matmul(p[1][0:128, 0:1], ones_r[:], tot2[:], start=True, stop=True)
            t2 = tg(p[5][0:1, 0:1])
            V.wait_ge(ct.h, t2)
            V.tensor_scalar_mul(half1[:], p[0][0:128, 0:1], 0.5)
            V.tensor_scalar_mul(nhalf1[:], p[0][0:128, 0:1], -0.5)
            V.tensor_scalar_mul(half2[:], p[1][0:128, 0:1], 0.5)
            g = cv.inc(V.tensor_scalar_mul(nhalf2[:], p[1][0:128, 0:1], -0.5))
            V.wait_ge(cv.h, g)
            G.wait_ge(cv.h, g)
            scal_ready_v = cv.v

            w1t_done = []
            w1scr_dma = []
            w1B_dv = []
            w2t_done = []
            w2scr_dma = []
            w2B_dv = []
            SY.wait_ge(cs.h, w1A_red_done[-1])
            SY.wait_ge(cv.h, w2A_red_done[-1])
            for i in range(PF):
                issue_w1(i, w1B_dv)
                issue_w2(i, w2B_dv)
            for j in range(nW):
                i = j + PF
                if i < nW:
                    if i >= 7:
                        SY.wait_ge(cv.h, w1t_done[i - 7])
                        SY.wait_ge(cv.h, w2t_done[i - 7])
                    issue_w1(i, w1B_dv)
                    issue_w2(i, w2B_dv)
                rj, sj = w1slices[j]
                V.wait_ge(swr.h, w1B_dv[min(j + 3, nW - 1)])
                if j == nW - 1:
                    V.wait_ge(sx.h, 32)
                    d0 = cv.inc(V.tensor_copy(out=tmpb1[0][:], in_=x[0][:]))
                    V.wait_ge(cv.h, d0)
                if j >= 3:
                    V.wait_ge(sscr.h, w1scr_dma[j - 3])
                V.tensor_single_scalar(tmpb1[j % 2][:], w1raw[j % 7][:],
                                       nhalf1[:], ALU.is_le)
                w1t_done.append(cv.inc(V.scalar_tensor_tensor(
                    out=t16a[j % 3][:], in0=w1raw[j % 7][:], scalar=half1[:],
                    in1=tmpb1[j % 2][:], op0=ALU.is_ge, op1=ALU.subtract)))
                G.wait_ge(cv.h, w1t_done[j])
                w1scr_dma.append(sscr.inc(G.dma_start(
                    out=w1scr[128 * rj:128 * (rj + 1), 1024 * sj:1024 * (sj + 1)],
                    in_=t16a[j % 3][:]), 16))
                V.wait_ge(swr.h, w2B_dv[min(j + 3, nW - 1)])
                if j == nW - 1:
                    V.wait_ge(sx.h, 32)
                    d0 = cv.inc(V.tensor_copy(out=tmpb2[0][:], in_=x[0][:]))
                    V.wait_ge(cv.h, d0)
                if j >= 3:
                    V.wait_ge(sscr.h, w2scr_dma[j - 3])
                V.tensor_single_scalar(tmpb2[j % 2][:], w2raw[j % 7][:],
                                       nhalf2[:], ALU.is_le)
                w2t_done.append(cv.inc(V.scalar_tensor_tensor(
                    out=t16b[j % 3][:], in0=w2raw[j % 7][:], scalar=half2[:],
                    in1=tmpb2[j % 2][:], op0=ALU.is_ge, op1=ALU.subtract)))
                G.wait_ge(cv.h, w2t_done[j])
                w2scr_dma.append(sscr.inc(G.dma_start(
                    out=w2scr[128 * j:128 * (j + 1), :],
                    in_=t16b[j % 3][:]), 16))
            w1scr_all = w1scr_dma[-1]
            w2scr_all = w2scr_dma[-1]
            tern_done = (cv.v, cg.v, sscr.v)

        stats1_ready = emit_stats(list(range(NXT)), nmu1, rsd1, wait_x)


        if KDUMP:
            V.tensor_copy(out=d_stA_sb[:, 0:17], in_=nmu1[:])
            gC = cv.inc(V.tensor_copy(out=d_stA_sb[:, 17:34], in_=rsd1[:]))
            SY.wait_ge(cv.h, gC)
            sout.inc(SY.dma_start(out=d_stC, in_=d_stA_sb[:]), 16)
        if KDUMP:
            V.wait_ge(cv.h, scal_ready_v)
            V.tensor_copy(out=rtmp[:, 0:1], in_=half1[:])
            V.tensor_copy(out=rtmp[:, 1:2], in_=nhalf1[:])
            V.tensor_copy(out=rtmp[:, 2:3], in_=half2[:])
            V.tensor_copy(out=rtmp[:, 3:4], in_=nhalf2[:])
            V.wait_ge(cv.h, stats1_ready)
            V.tensor_copy(out=rtmp[:, 4:5], in_=nmu1[:, 0:1])
            V.tensor_copy(out=rtmp[:, 5:6], in_=rsd1[:, 0:1])
            V.tensor_copy(out=rtmp[:, 6:7], in_=nmu1[:, 16:17])
            gdv = cv.inc(V.tensor_copy(out=rtmp[:, 7:8], in_=rsd1[:, 16:17]))
            SY.wait_ge(cv.h, gdv)
            sout.inc(SY.dma_start(out=d_tern, in_=rtmp[:, 0:8]), 16)
        import os as _os
        if _os.environ.get("KDEBUG"):
            print("sbuf before ffn:", nc.sbuf_bytes_remaining)
        KSER = _os.environ.get("KSER", "")
        w1tl_stash = []
        def emit_ffn(tiles, nmu_t, rsd_t, stats_ready, xtile_wait, is_ffn2,
                     stats_cb=None):
            groups = []
            i = 0
            while i < len(tiles):
                groups.append(tiles[i:i + 2])
                i += 2
            with ExitStack() as es:
                w1tl = [es.enter_context(nc.sbuf_tensor(f"w1tl{i}_{is_ffn2}", [128, FF], BF16))
                        for i in range(NKD)]
                w1tl_stash.clear(); w1tl_stash.extend(w1tl)
                w2T = [es.enter_context(nc.sbuf_tensor(f"w2T{i}_{is_ffn2}", [128, D], BF16))
                       for i in range(5)]
                ybuf = [es.enter_context(nc.sbuf_tensor(f"ybuf{i}_{is_ffn2}", [128, D], F32))
                        for i in range(1)]
                yhi = [[es.enter_context(nc.sbuf_tensor(f"yhi{s}_{i}_{is_ffn2}", [128, 256], BF16))
                        for i in range(NKD)] for s in range(2)]
                ylo = [[es.enter_context(nc.sbuf_tensor(f"ylo{s}_{i}_{is_ffn2}", [128, 256], BF16))
                        for i in range(NKD)] for s in range(2)]
                f1f = [es.enter_context(nc.sbuf_tensor(f"f1f{i}_{is_ffn2}", [128, 256], F32))
                       for i in range(2)]
                f1h = [es.enter_context(nc.sbuf_tensor(f"f1h{i}_{is_ffn2}", [128, 256], BF16))
                       for i in range(2)]
                f1l = [es.enter_context(nc.sbuf_tensor(f"f1l{i}_{is_ffn2}", [128, 256], BF16))
                       for i in range(2)]

                SY.wait_ge(sscr.h, w2scr_all)
                V.wait_ge(sscr.h, w2scr_all)
                S.wait_ge(sscr.h, w2scr_all)
                w1tl_dma = []
                for k in range(NKD):
                    for q_ in range(4):
                        cq = slice(1024 * q_, 1024 * (q_ + 1))
                        w1tl_dma.append(swt.inc(SY.dma_start(
                            out=w1tl[k][:, cq],
                            in_=w1scr[128 * k:128 * (k + 1), cq]), 16))

                stats_pending = []
                w2T_load = {}
                w2T_free = {}
                mm1_free = {}
                split_free = {}
                ybuf_free = {}
                tr_free = {}
                out2_free = {}
                ev_done = {}
                fidx = 0

                split_ready = {}

                def emit_prep_tile(g, lt):
                    txs_g = groups[g]
                    stg = g % 2
                    j = txs_g[lt]
                    jj = tiles.index(j)
                    V.wait_ge(cv.h, stats_ready)
                    if stg in split_free:
                        S.wait_ge(ct.h, split_free[stg])
                        V.wait_ge(ct.h, split_free[stg])
                    xtile_wait(V, j)
                    if ybuf_free:
                        V.wait_ge(ct.h, max(ybuf_free.values()))
                    yv = cv.inc(V.tensor_scalar(
                        out=ybuf[0][:], in0=x[j][:],
                        scalar1=nmu_t[:, jj:jj + 1],
                        scalar2=rsd_t[:, jj:jj + 1],
                        op0=ALU.add, op1=ALU.mult))
                    for w in (0, 1):
                        bank = 6 + w
                        T.wait_ge(cv.h, yv)
                        if bank in tr_free:
                            T.wait_ge(cv.h, tr_free[bank])
                        for c in range(4 * w, 4 * w + 4):
                            sl = (c % 4) * 128
                            last = T.transpose(
                                p[bank][0:128, sl:sl + 128],
                                ybuf[0][:, 128 * c:128 * (c + 1)], ident[:])
                        tv = ct.inc(last)
                        ybuf_free[(g, lt)] = tv
                        S.wait_ge(ct.h, tv)
                        for c in range(4 * w, 4 * w + 4):
                            sl = (c % 4) * 128
                            last = S.activation(
                                out=yhi[stg][c][:, 128 * lt:128 * lt + 128],
                                in_=p[bank][0:128, sl:sl + 128],
                                func=AF.Identity, bias=0.0, scale=1.0)
                        sv = cs.inc(last)
                        V.wait_ge(cs.h, sv)
                        for c in range(4 * w, 4 * w + 4):
                            sl = (c % 4) * 128
                            last = V.tensor_sub(
                                ylo[stg][c][:, 128 * lt:128 * lt + 128],
                                p[bank][0:128, sl:sl + 128],
                                yhi[stg][c][:, 128 * lt:128 * lt + 128])
                        tr_free[bank] = cv.inc(last)
                    if lt == len(txs_g) - 1:
                        split_ready[stg] = (cs.v, cv.v)

                for lt_ in range(len(groups[0])):
                    emit_prep_tile(0, lt_)
                if KDUMP and not is_ffn2:
                    SY.wait_ge(cs.h, split_ready[0][0])
                    SY.wait_ge(cv.h, split_ready[0][1])
                    SY.wait_ge(swt.h, w1tl_dma[-1])
                    sout.inc(SY.dma_start(out=d_w1tl, in_=w1tl[0][:]), 16)
                    sout.inc(SY.dma_start(out=d_yhi, in_=yhi[0][0][:]), 16)
                    sout.inc(SY.dma_start(out=d_ylo, in_=ylo[0][0][:]), 16)

                for gi, txs in enumerate(groups):
                    NT = 128 * len(txs)
                    st = gi % 2
                    lov_of = {}

                    ngq = 32 * len(groups)

                    def emit_mm2(fq, gq):
                        T.wait_ge(cv.h, lov_of[fq])
                        T.wait_ge(swt.h, w2T_load[min(gq + 2, ngq - 1)])
                        if fq == 0:
                            for b in range(2 * len(txs)):
                                if b in out2_free:
                                    T.wait_ge(cv.h, out2_free[b])
                        fbq = fq % 2
                        for lt in range(len(txs)):
                            for sidx, srcq in enumerate((f1h, f1l)):
                                for half in range(2):
                                    lastq = T.matmul(
                                        p[2 * lt + half][0:128, 0:512],
                                        srcq[fbq][:, 128 * lt:128 * lt + 128],
                                        w2T[gq % 5][:, 512 * half:512 * (half + 1)],
                                        start=(fq == 0 and sidx == 0),
                                        stop=False)
                        w2T_free[gq] = ct.inc(lastq)
                        nq = gq + 5
                        if nq < ngq:
                            fq3 = nq % NKF
                            SY.wait_ge(ct.h, w2T_free[gq])
                            w2T_load[nq] = swt.inc(SY.dma_start(
                                out=w2T[nq % 5][:],
                                in_=w2scr[128 * fq3:128 * (fq3 + 1), :]), 16)

                    for f in range(NKF):
                        if fidx == 0:
                            for q_ in range(5):
                                w2T_load[q_] = swt.inc(SY.dma_start(
                                    out=w2T[q_ % 5][:],
                                    in_=w2scr[128 * q_:128 * (q_ + 1), :]), 16)
                        bank = 4 + (f % 2)
                        if f == 0:
                            T.wait_ge(swt.h, w1tl_dma[-1])
                            T.wait_ge(cs.h, split_ready[st][0])
                            T.wait_ge(cv.h, split_ready[st][1])
                        if bank in mm1_free:
                            T.wait_ge(cs.h, mm1_free[bank])
                        nmm = 0
                        for kk in range(NKD):
                            for src in (yhi, ylo):
                                last = T.matmul(
                                    p[bank][0:128, 0:NT],
                                    w1tl[kk][:, 128 * f:128 * (f + 1)],
                                    src[st][kk][:, 0:NT],
                                    start=(nmm == 0), stop=(nmm == 2 * NKD - 1))
                                nmm += 1
                        mm1v = ct.inc(last)
                        if f == NKF - 1:
                            split_free[st] = mm1v
                        fb = f % 2
                        S.wait_ge(ct.h, mm1v)
                        S.activation(out=f1f[fb][:, 0:NT], in_=p[bank][0:128, 0:NT],
                                     func=AF.Gelu, bias=b1f_sb[:, f:f + 1], scale=1.0)
                        mm1_free[bank] = cs.inc(S.activation(
                            out=f1h[fb][:, 0:NT], in_=f1f[fb][:, 0:NT],
                            func=AF.Identity, bias=0.0, scale=1.0))
                        V.wait_ge(cs.h, mm1_free[bank])
                        lov_of[f] = cv.inc(V.tensor_sub(f1l[fb][:, 0:NT],
                                                        f1f[fb][:, 0:NT],
                                                        f1h[fb][:, 0:NT]))
                        if f == 8 and gi + 1 < len(groups):
                            emit_prep_tile(gi + 1, 0)
                        if f == 20 and gi + 1 < len(groups) \
                                and len(groups[gi + 1]) > 1:
                            emit_prep_tile(gi + 1, 1)
                        if f in (14, 26) and stats_pending:
                            stats_cb(stats_pending.pop(0))
                        if f >= 1:
                            emit_mm2(f - 1, fidx - 1)
                        fidx += 1
                    emit_mm2(NKF - 1, fidx - 1)
                    for lt in range(len(txs)):
                        for half in range(2):
                            last = T.matmul(
                                p[2 * lt + half][0:128, 0:512],
                                ones_r[0:1, 0:128],
                                b2f_sb[0:1, 512 * half:512 * (half + 1)],
                                start=False, stop=True)
                    bv_ = ct.inc(last)
                    if "c" in KSER:
                        barrier([V, S, T, G, SY])
                    V.wait_ge(ct.h, bv_)
                    for lt, j in enumerate(txs):
                        for half in range(2):
                            hc = slice(512 * half, 512 * (half + 1))
                            last = V.scalar_tensor_tensor(
                                out=x[j][:, hc], in0=p[2 * lt + half][0:128, 0:512],
                                scalar=0.5, in1=x[j][:, hc],
                                op0=ALU.mult, op1=ALU.add)
                            out2_free[2 * lt + half] = cv.inc(last)
                        ev_done[j] = cv.v
                        if stats_cb is not None:
                            stats_pending.append(j)
                        if is_ffn2:
                            SY.wait_ge(cv.h, ev_done[j])
                            sout.inc(SY.dma_start(
                                out=xout[128 * (j - 1):128 * j, 0:512],
                                in_=x[j][:, 0:512]), 16)
                            sout.inc(SY.dma_start(
                                out=xout[128 * (j - 1):128 * j, 512:1024],
                                in_=x[j][:, 512:1024]), 16)
                while stats_pending:
                    stats_cb(stats_pending.pop(0))
                return ev_done

        def wait_none(V_, j):
            pass
        for e_ in (T, S, V, G):
            e_.wait_ge(swa.h, swa_done)
        st2_cp = {}

        def cb_stats2(j):
            st2_cp["cp"] = emit_stats_tile(j, j, nmu2)
        ffn1_ev = emit_ffn(list(range(NXT)), nmu1, rsd1, stats1_ready,
                           wait_none, False, stats_cb=cb_stats2)

        KFFN1 = bool(_os.environ.get("KFFN1"))
        if KFFN1:
            for k in range(NKD):
                sout.inc(SY.dma_start(out=d_w1tl8[k], in_=w1tl_stash[k][:]), 16)
            V.wait_ge(cv.h, stats1_ready)
            V.tensor_copy(out=rtmp[:, 0:17], in_=nmu1[:])
            gS = cv.inc(V.tensor_copy(out=rtmp[:, 17:32], in_=rsd1[:, 0:15]))
            SY.wait_ge(cv.h, gS)
            sout.inc(SY.dma_start(out=d_st[0:128, 0:32], in_=rtmp[:]), 16)
            for j in range(1, NXT):
                SY.wait_ge(cv.h, ffn1_ev[j])
                sout.inc(SY.dma_start(out=xout[128 * (j - 1):128 * j, :],
                                      in_=x[j][:]), 16)
        if not KFFN1:
            barrier([V, S, T, G, SY])
            if KDUMP:
                for j in range(NXT):
                    sout.inc(SY.dma_start(out=d_x1[128 * j:128 * (j + 1), :],
                                          in_=x[j][:]), 16)
            stats2_ready = emit_stats_fin(NXT, rsd2, st2_cp["cp"])
            V.tensor_scalar_add(a2t[:], rsd2[:], 1.0)
            g = cv.inc(V.tensor_mul(b2ct[:], nmu2[:], rsd2[:]))
            V.wait_ge(cv.h, g)
            zadd_ready = cv.v
            st3_cp = {}

            with ExitStack() as es:
                wi_sb = [es.enter_context(nc.sbuf_tensor(f"wi{k}", [128, DS], F32))
                         for k in range(NKD)]
                w1a_sb = [es.enter_context(nc.sbuf_tensor(f"w1a{k}", [128, DS], F32))
                          for k in range(NKD)]
                for k in range(NKD):
                    swa.inc(G.dma_start(out=wi_sb[k][:],
                                        in_=wi_d[128 * k:128 * (k + 1), :]), 16)
                    swa.inc(G.dma_start(out=w1a_sb[k][:],
                                        in_=w1a_d[128 * k:128 * (k + 1), :]), 16)
                T.wait_ge(swa.h, swa.v)
                hA = [es.enter_context(nc.sbuf_tensor(f"hA{i}", [128, TH], F32))
                      for i in range(2)]
                hB = [es.enter_context(nc.sbuf_tensor(f"hB{i}", [128, 2112], F32))
                      for i in range(2)]
                zbuf = es.enter_context(nc.sbuf_tensor("zbuf", [128, D], F32))
                zT0 = [es.enter_context(nc.sbuf_tensor(f"zT{i}", [128, 128], F32))
                       for i in range(NKD)]
                k_g = [es.enter_context(nc.sbuf_tensor(f"k_g{i}", [128, 576], F32))
                       for i in range(2)]
                v_g = [es.enter_context(nc.sbuf_tensor(f"v_g{i}", [64, 256], F32))
                       for i in range(9)]
                q_g = [es.enter_context(nc.sbuf_tensor(f"q_g{i}", [128, 512], F32))
                       for i in range(2)]
                mx_g = [es.enter_context(nc.sbuf_tensor(f"mx_g{i}", [128, 512], F32))
                        for i in range(2)]
                sm_g = [es.enter_context(nc.sbuf_tensor(f"sm_g{i}", [128, 512], F32))
                        for i in range(2)]
                th_g = [es.enter_context(nc.sbuf_tensor(f"th_g{i}", [128, 512], F32))
                        for i in range(2)]
                d1_g = [es.enter_context(nc.sbuf_tensor(f"d1_g{i}", [128, 512], F32))
                        for i in range(2)]
                sc_sb = [es.enter_context(nc.sbuf_tensor(f"sc_sb{i}", [128, 192], F32))
                         for i in range(2)]
                ef_sb = [es.enter_context(nc.sbuf_tensor(f"ef_sb{i}", [128, 192], F32))
                         for i in range(2)]
                mk_sb = [es.enter_context(nc.sbuf_tensor(f"mk_sb{i}", [128, 192], F32))
                         for i in range(2)]
                top8 = [es.enter_context(nc.sbuf_tensor(f"top8_{i}", [128, 8], F32))
                        for i in range(2)]
                negm = [es.enter_context(nc.sbuf_tensor(f"negm{i}", [128, 1], F32))
                        for i in range(2)]
                den = [es.enter_context(nc.sbuf_tensor(f"den{i}", [128, 1], F32))
                       for i in range(2)]
                rd_ = [es.enter_context(nc.sbuf_tensor(f"rd_{i}", [128, 1], F32))
                       for i in range(2)]
                wT64 = [[es.enter_context(nc.sbuf_tensor(f"wT64_{s}_{i}", [64, 128], F32))
                         for i in range(3)] for s in range(2)]
                ph_sb = [es.enter_context(nc.sbuf_tensor(f"ph_sb{i}", [128, 136], F32))
                         for i in range(2)]
                pb_sb = [es.enter_context(nc.sbuf_tensor(f"pb_sb{i}", [128, 136], F32))
                         for i in range(2)]
                gate_row = es.enter_context(
                    nc.sbuf_tensor("gate_row", [1, 2112], F32))

                zT = [[t[:] for t in zT0],
                      [th_g[c // 4][:, 128 * (c % 4):128 * (c % 4) + 128]
                       for c in range(NKD)]]
                h0_mark = {}

                def emit_h0_mm(jq):
                    T.wait_ge(cs.h, h0_mark[(jq, "ev")])
                    if jq >= 1:
                        T.wait_ge(cs.h, h0_mark[(jq - 1, "h")])
                    for m in range(2):
                        for kk in range(NKD):
                            last = T.matmul(p[m][0:128, 0:128],
                                            wi_sb[kk][:, 128 * m:128 * (m + 1)],
                                            zT[jq % 2][kk], start=(kk == 0),
                                            stop=(kk == NKD - 1))
                    mmv = tg(p[5][0:1, 0:1])
                    h0_mark[(jq, "mm")] = mmv
                    S.wait_ge(ct.h, mmv)
                    for m in range(2):
                        last = S.activation(out=hA[m][:, 128 * jq:128 * (jq + 1)],
                                            in_=p[m][0:128, 0:128],
                                            func=AF.Identity,
                                            bias=bias2["bi_t"][:, m:m + 1], scale=1.0)
                    h0_mark[(jq, "h")] = cs.inc(last)

                for j in range(NXT):
                    V.wait_ge(cv.h, stats2_ready)
                    if j >= 1:
                        V.wait_ge(ct.h, h0_mark[(j - 1, "tr")])
                    zv = cv.inc(V.tensor_scalar(
                        out=zbuf[:], in0=x[j][:], scalar1=nmu2[:, j:j + 1],
                        scalar2=rsd2[:, j:j + 1], op0=ALU.add, op1=ALU.mult))
                    T.wait_ge(cv.h, zv)
                    if j >= 1:
                        T.wait_ge(cs.h, h0_mark[(j - 1, "ev")])
                    for c in range(NKD):
                        bank = 6 + (c // 4)
                        sl = (c % 4) * 128
                        last = T.transpose(p[bank][0:128, sl:sl + 128],
                                           zbuf[:, 128 * c:128 * (c + 1)], ident[:])
                    h0_mark[(j, "tr")] = tg(p[5][0:1, 0:1])
                    S.wait_ge(ct.h, h0_mark[(j, "tr")])
                    if j >= 2:
                        S.wait_ge(ct.h, h0_mark[(j - 2, "mm")])
                    for c in range(NKD):
                        bank = 6 + (c // 4)
                        sl = (c % 4) * 128
                        last = S.copy(out=zT[j % 2][c],
                                      in_=p[bank][0:128, sl:sl + 128])
                    h0_mark[(j, "ev")] = cs.inc(last)
                    if j >= 1:
                        emit_h0_mm(j - 1)
                emit_h0_mm(NXT - 1)
                V.wait_ge(cs.h, h0_mark[(NXT - 1, "h")])
                V.tensor_single_scalar(hA[0][:, 0:128], hA[0][:, 0:128], hscal[:], ALU.mult)
                g = cv.inc(V.tensor_single_scalar(hA[1][:, 0:128], hA[1][:, 0:128],
                                                  hscal[:], ALU.mult))
                h0_ready = (cs.v, cv.v)

                def emit_attn_step(IN, OUT, Sn, blocks, blk0_ob, tok0_col, in_ready):
                    Tq = Sn - 64
                    nch = Sn // 16
                    T.wait_ge(cs.h, in_ready[0])
                    T.wait_ge(cv.h, in_ready[1])
                    V.wait_ge(cs.h, in_ready[0])
                    S.wait_ge(cv.h, in_ready[1])
                    for c in range(2):
                        x3 = IN[c][:, 0:Sn].rearrange("p (a b) -> p a b", b=16)
                        last = V.reduce_sum(ph_sb[c][:, 0:nch], x3, axis=AX.X)
                    g = cv.inc(last)
                    V.wait_ge(cv.h, g)
                    T.wait_ge(cv.h, g)
                    for m in range(2):
                        for kk in range(2):
                            last = T.matmul(p[4][0:128, 0:nch],
                                            wp_sb[kk][:, 128 * m:128 * (m + 1)],
                                            ph_sb[kk][:, 0:nch],
                                            start=(kk == 0), stop=(kk == 1))
                        tv = tg(p[5][0:1, 0:1])
                        S.wait_ge(ct.h, tv)
                        last = S.activation(out=pb_sb[m][:, 0:nch],
                                            in_=p[4][0:128, 0:nch], func=AF.Identity,
                                            bias=bias2["bp_t"][:, m:m + 1],
                                            scale=sc16[:, 0:1])
                        pbv = cs.inc(last)
                        T.wait_ge(cs.h, pbv)
                    pb_ready = cs.v
                    for t0 in range(0, Tq, 512):
                        n = min(512, Tq - t0)
                        for kk in range(2):
                            last = T.matmul(p[4][0:1, 0:n],
                                            wh_sb[kk][:, 2:3],
                                            IN[kk][:, 64 + t0:64 + t0 + n],
                                            start=(kk == 0), stop=(kk == 1))
                        tv = tg(p[5][0:1, 0:1])
                        S.wait_ge(ct.h, tv)
                        gv = cs.inc(S.activation(out=gate_row[0:1, t0:t0 + n],
                                                 in_=p[4][0:1, 0:n], func=AF.Sigmoid,
                                                 bias=bh_sb[0:1, 2:3], scale=1.0))
                        T.wait_ge(cs.h, gv)
                    gate_ready = cs.v

                    groups = [blocks[i:i + 4] for i in range(0, len(blocks), 4)]
                    out_marks = []
                    sm_ev_par = {}
                    sc_ev_par = {}
                    wt_ev_par = {}
                    prev_d1v = None
                    for grp in groups:
                        gb0 = grp[0][0]
                        gw = sum(q for _, q in grp)
                        cw_ = gw + 64
                        qc = slice(64 + gb0, 64 + gb0 + gw)
                        if 1 in sc_ev_par:
                            T.wait_ge(cv.h, sc_ev_par[1])
                        pos = 0
                        while pos < cw_:
                            n = min(512, cw_ - pos)
                            for m in range(2):
                                for kk in range(2):
                                    last = T.matmul(
                                        p[m][0:128, 0:n],
                                        wk_sb[kk][:, 128 * m:128 * (m + 1)],
                                        IN[kk][:, gb0 + pos:gb0 + pos + n],
                                        start=(kk == 0), stop=(kk == 1))
                                tv = tg(p[5][0:1, 0:1])
                                S.wait_ge(ct.h, tv)
                                ev = cs.inc(S.activation(
                                    out=k_g[m][:, pos:pos + n], in_=p[m][0:128, 0:n],
                                    func=AF.Identity,
                                    bias=bias2["bk_t"][:, m:m + 1], scale=1.0))
                                T.wait_ge(cs.h, ev)
                            pos += n
                        nv = cw_ // 64
                        if prev_d1v is not None:
                            T.wait_ge(cs.h, prev_d1v)
                        for t in range(nv):
                            bank = 2 + (t % 2)
                            for kk in range(2):
                                last = T.matmul(
                                    p[bank][0:64, 0:256],
                                    IN[kk][:, gb0 + 64 * t:gb0 + 64 * (t + 1)],
                                    wv_sb[kk][:], start=(kk == 0), stop=(kk == 1))
                            tv = tg(p[5][0:1, 0:1])
                            S.wait_ge(ct.h, tv)
                            ev = cs.inc(S.copy(out=v_g[t][:], in_=p[bank][0:64, 0:256]))
                            T.wait_ge(cs.h, ev)
                        kv_ready = cs.v
                        for m in range(2):
                            for kk in range(2):
                                last = T.matmul(
                                    p[m][0:128, 0:gw],
                                    wq_sb[kk][:, 128 * m:128 * (m + 1)],
                                    IN[kk][:, qc], start=(kk == 0), stop=(kk == 1))
                            tv = tg(p[5][0:1, 0:1])
                            S.wait_ge(ct.h, tv)
                            ev = cs.inc(S.activation(
                                out=q_g[m][:, 0:gw], in_=p[m][0:128, 0:gw],
                                func=AF.Identity, bias=bias2["bq_t"][:, m:m + 1],
                                scale=1.0))
                            T.wait_ge(cs.h, ev)
                        q_ready = cs.v
                        V.wait_ge(cs.h, q_ready)
                        for c in range(2):
                            base = 64 + gb0 - 4
                            V.tensor_scalar(out=mx_g[c][:, 0:gw],
                                            in0=IN[c][:, base:base + gw],
                                            scalar1=cw_sb[:, 5 * c:5 * c + 1],
                                            scalar2=bias2["cb_t"][:, c:c + 1],
                                            op0=ALU.mult, op1=ALU.add)
                            for kk in range(1, 5):
                                last = V.scalar_tensor_tensor(
                                    out=mx_g[c][:, 0:gw],
                                    in0=IN[c][:, base + kk:base + kk + gw],
                                    scalar=cw_sb[:, 5 * c + kk:5 * c + kk + 1],
                                    in1=mx_g[c][:, 0:gw],
                                    op0=ALU.mult, op1=ALU.add)
                        mx_ready = cv.inc(last)

                        scv_of = {}

                        def emit_scores(bi_):
                            ob, qlen = grp[bi_]
                            par = bi_ % 2
                            lc = ob - gb0
                            cl = qlen + 64
                            sb = p[4] if par == 0 else p[0]
                            if par in sc_ev_par:
                                T.wait_ge(cv.h, sc_ev_par[par])
                            for kk in range(2):
                                last = T.matmul(
                                    sb[0:qlen, 0:cl],
                                    q_g[kk][:, lc:lc + qlen],
                                    k_g[kk][:, lc:lc + cl],
                                    start=(kk == 0), stop=(kk == 1))
                            scv_of[bi_] = tg(p[5][0:1, 0:1])

                        emit_scores(0)
                        for bi_, (ob, qlen) in enumerate(grp):
                            par = bi_ % 2
                            lc = ob - gb0
                            cl = qlen + 64
                            mask = blk0m if ob == blk0_ob else bandm
                            sb = p[4] if par == 0 else p[0]
                            if bi_ + 1 < len(grp):
                                emit_scores(bi_ + 1)
                            V.wait_ge(ct.h, scv_of[bi_])
                            sc_ev_par[par] = cv.inc(V.scalar_tensor_tensor(
                                out=sc_sb[par][0:qlen, 0:cl],
                                in0=sb[0:qlen, 0:cl], scalar=1.0 / 16.0,
                                in1=mask[0:qlen, 0:cl], op0=ALU.mult, op1=ALU.add))
                            g8 = cv.inc(V.max(out=top8[par][0:qlen, :],
                                              in_=sc_sb[par][0:qlen, 0:cl]))
                            V.wait_ge(cv.h, g8)
                            nmv = cv.inc(V.tensor_scalar_mul(
                                negm[par][0:qlen, :], top8[par][0:qlen, 0:1], -1.0))
                            S.wait_ge(cv.h, nmv)
                            S.activation(out=ef_sb[par][0:qlen, 0:8],
                                         in_=top8[par][0:qlen, :], func=AF.Exp,
                                         bias=negm[par][0:qlen, :], scale=1.0,
                                         accum_out=den[par][0:qlen, :])
                            exv = cs.inc(S.activation(
                                out=ef_sb[par][0:qlen, 0:cl],
                                in_=sc_sb[par][0:qlen, 0:cl], func=AF.Exp,
                                bias=negm[par][0:qlen, :], scale=1.0))
                            V.wait_ge(cs.h, exv)
                            rdv = cv.inc(V.reciprocal(rd_[par][0:qlen, :],
                                                      den[par][0:qlen, :]))
                            V.wait_ge(cv.h, rdv)
                            V.tensor_single_scalar(mk_sb[par][0:qlen, 0:cl],
                                                   sc_sb[par][0:qlen, 0:cl],
                                                   top8[par][0:qlen, 7:8], ALU.is_ge)
                            wfv = cv.inc(V.scalar_tensor_tensor(
                                out=ef_sb[par][0:qlen, 0:cl],
                                in0=ef_sb[par][0:qlen, 0:cl],
                                scalar=rd_[par][0:qlen, :],
                                in1=mk_sb[par][0:qlen, 0:cl],
                                op0=ALU.mult, op1=ALU.mult))
                            T.wait_ge(cv.h, wfv)
                            if par in wt_ev_par:
                                T.wait_ge(cs.h, wt_ev_par[par])
                            npieces = cl // 64
                            for pi in range(npieces):
                                sl = pi * 128
                                last = T.transpose(
                                    p[6 + par][0:64, sl:sl + qlen],
                                    ef_sb[par][0:qlen, 64 * pi:64 * (pi + 1)],
                                    ident[0:qlen, 0:qlen])
                            trv = tg(p[5][0:1, 0:1])
                            S.wait_ge(ct.h, trv)
                            for pi in range(npieces):
                                sl = pi * 128
                                last = S.copy(out=wT64[par][pi][0:64, 0:qlen],
                                              in_=p[6 + par][0:64, sl:sl + qlen])
                            wev = cs.inc(last)
                            wt_ev_par[par] = wev
                            T.wait_ge(cs.h, wev)
                            for m in range(2):
                                off = m * 128
                                for pi in range(npieces):
                                    vt = lc // 64 + pi
                                    last = T.matmul(
                                        p[2 + par][0:128, off:off + qlen],
                                        v_g[vt][0:64, 128 * m:128 * (m + 1)],
                                        wT64[par][pi][0:64, 0:qlen],
                                        start=(pi == 0), stop=(pi == npieces - 1))
                            smv = tg(p[5][0:1, 0:1])
                            S.wait_ge(ct.h, smv)
                            for m in range(2):
                                off = m * 128
                                last = S.activation(
                                    out=sm_g[m][:, lc:lc + qlen],
                                    in_=p[2 + par][0:128, off:off + qlen],
                                    func=AF.Identity,
                                    bias=bias2["bv_t"][:, m:m + 1], scale=1.0)
                            smev = cs.inc(last)
                            sm_ev_par[par] = smev
                            if tok0_col is not None and ob == blk0_ob:
                                c0 = tok0_col - gb0
                                V.wait_ge(cs.h, smev)
                                V.tensor_single_scalar(sm_g[0][:, c0:c0 + 1],
                                                       sm_g[0][:, c0:c0 + 1],
                                                       t0scal[:], ALU.mult)
                                cv.inc(V.tensor_single_scalar(
                                    sm_g[1][:, c0:c0 + 1], sm_g[1][:, c0:c0 + 1],
                                    t0scal[:], ALU.mult))
                        summ_ready = (cs.v, cv.v)

                        T.wait_ge(cs.h, summ_ready[0])
                        T.wait_ge(cv.h, summ_ready[1])
                        T.wait_ge(cv.h, mx_ready)
                        T.wait_ge(cs.h, pb_ready)
                        pbc0 = (64 + gb0) // 16
                        rhs_list = [IN[0][:, qc], IN[1][:, qc],
                                    mx_g[0][:, 0:gw], mx_g[1][:, 0:gw],
                                    pb_sb[0][:, pbc0:pbc0 + gw // 16]
                                    .unsqueeze(2).broadcast_to([128, gw // 16, 16]),
                                    pb_sb[1][:, pbc0:pbc0 + gw // 16]
                                    .unsqueeze(2).broadcast_to([128, gw // 16, 16]),
                                    sm_g[0][:, 0:gw], sm_g[1][:, 0:gw]]
                        for m in range(2):
                            for kk in range(NKD):
                                last = T.matmul(p[m][0:128, 0:gw],
                                                w1a_sb[kk][:, 128 * m:128 * (m + 1)],
                                                rhs_list[kk], start=(kk == 0),
                                                stop=(kk == NKD - 1))
                            uiv = tg(p[5][0:1, 0:1])
                            S.wait_ge(ct.h, uiv)
                            thv = cs.inc(S.activation(
                                out=th_g[m][:, 0:gw], in_=p[m][0:128, 0:gw],
                                func=AF.Tanh, bias=bias2["b1a_t"][:, m:m + 1],
                                scale=1.0))
                            T.wait_ge(cs.h, thv)
                        for m in range(2):
                            for kk in range(2):
                                last = T.matmul(p[2 + m][0:128, 0:gw],
                                                w2a_sb[kk][:, 128 * m:128 * (m + 1)],
                                                th_g[kk][:, 0:gw], start=(kk == 0),
                                                stop=(kk == 1))
                        dv = tg(p[5][0:1, 0:1])
                        T.wait_ge(cs.h, gate_ready)
                        gbv = ct.inc(T.matmul(p[6][0:128, 0:gw], ones_r[:],
                                              gate_row[0:1, gb0:gb0 + gw],
                                              start=True, stop=True))
                        S.wait_ge(ct.h, dv)
                        for m in range(2):
                            last = S.activation(out=d1_g[m][:, 0:gw],
                                                in_=p[2 + m][0:128, 0:gw],
                                                func=AF.Identity,
                                                bias=bias2["b2a_t"][:, m:m + 1],
                                                scale=1.0)
                        d1v = cs.inc(last)
                        prev_d1v = d1v
                        V.wait_ge(cs.h, d1v)
                        V.wait_ge(ct.h, gbv)
                        for m in range(2):
                            V.tensor_mul(d1_g[m][:, 0:gw], d1_g[m][:, 0:gw],
                                         p[6][0:128, 0:gw])
                            last = V.tensor_add(OUT[m][:, gb0:gb0 + gw],
                                                d1_g[m][:, 0:gw], IN[m][:, qc])
                        out_marks.append(cv.inc(last))
                    return out_marks[-1]

                if KDUMP:
                    SY.wait_ge(cs.h, h0_ready[0])
                    SY.wait_ge(cv.h, h0_ready[1])
                    for m in range(2):
                        sout.inc(SY.dma_start(out=d_h0[m, :, :], in_=hA[m][:]), 16)
                blocks1 = [(0, 64)] + [(64 + 128 * b, 128) for b in range(16)]
                s1v = emit_attn_step(hA, hB, TH, blocks1, 64, 64, h0_ready)
                V.wait_ge(cv.h, s1v)
                V.tensor_single_scalar(hB[0][:, 0:64], hB[0][:, 0:64], hscal[:], ALU.mult)
                g = cv.inc(V.tensor_single_scalar(hB[1][:, 0:64], hB[1][:, 0:64],
                                                  hscal[:], ALU.mult))
                h1_ready = (cs.v, cv.v)

                if KDUMP:
                    SY.wait_ge(cs.h, h1_ready[0])
                    SY.wait_ge(cv.h, h1_ready[1])
                    for m in range(2):
                        sout.inc(SY.dma_start(out=d_h1[m, :, :], in_=hB[m][:]), 16)
                blocks2 = [(128 * b, 128) for b in range(16)]
                s2v = emit_attn_step(hB, hA, 2112, blocks2, 0, 0, h1_ready)

                if KDUMP:
                    SY.wait_ge(cv.h, s2v)
                    for m in range(2):
                        sout.inc(SY.dma_start(out=d_h2[m, :, :],
                                              in_=hA[m][:, 0:TOWN]), 16)
                x2_ev = {}
                for j in range(1, NXT):
                    hc = slice(128 * (j - 1), 128 * j)
                    bk = 2 * ((j - 1) % 2)
                    T.wait_ge(cv.h, s2v)
                    if j >= 3:
                        T.wait_ge(cv.h, x2_ev[j - 2])
                    for half in range(2):
                        for kk in range(2):
                            last = T.matmul(p[bk + half][0:128, 0:512],
                                            hA[kk][:, hc],
                                            wo_sb[kk][:, 512 * half:512 * (half + 1)],
                                            start=(kk == 0), stop=False)
                        last = T.matmul(p[bk + half][0:128, 0:512], ones_r[0:1, 0:128],
                                        bo_sb[0:1, 512 * half:512 * (half + 1)],
                                        start=False, stop=True)
                    uv = tg(p[5][0:1, 0:1])
                    V.wait_ge(ct.h, uv)
                    V.wait_ge(cv.h, zadd_ready)
                    V.tensor_scalar(out=zbuf[:], in0=x[j][:],
                                    scalar1=a2t[:, j:j + 1], scalar2=b2ct[:, j:j + 1],
                                    op0=ALU.mult, op1=ALU.add)
                    for half in range(2):
                        hcd = slice(512 * half, 512 * (half + 1))
                        last = V.scalar_tensor_tensor(
                            out=x[j][:, hcd], in0=p[bk + half][0:128, 0:512],
                            scalar=1.0,
                            in1=zbuf[:, hcd], op0=ALU.mult, op1=ALU.add)
                    x2_ev[j] = cv.inc(last)
                    st3_cp["cp"] = emit_stats_tile(j - 1, j, nmu3)

            if KDUMP:
                for j in range(1, NXT):
                    SY.wait_ge(cv.h, x2_ev[j])
                    sout.inc(SY.dma_start(out=d_x2[128 * j:128 * (j + 1), :],
                                          in_=x[j][:]), 16)
            barrier([V, S, T, G, SY])
            stats3_ready = emit_stats_fin(NXT - 1, rsd3, st3_cp["cp"])
            emit_ffn(list(range(1, NXT)), nmu3, rsd3, stats3_ready,
                     wait_none, True)

        SY.wait_ge(sout.h, sout.v)

    nc.finalize()
    return nc


_PROG = None


def _get_prog():
    global _PROG
    if _PROG is None:
        _PROG = build_program()
    return _PROG


def kernel(**inputs):
    f32 = np.float32
    P = {k: np.ascontiguousarray(np.asarray(v, f32)) for k, v in inputs.items()}
    x_full = P["L0_attn_x"].reshape(2 * 8192, D)

    def t2(b):
        return np.ascontiguousarray(b.reshape(2, 128).T)

    cw = P["L0_attn_conv_w"]
    cw_t = np.ascontiguousarray(
        np.transpose(cw.reshape(5, 2, 128), (2, 1, 0)).reshape(128, 10))
    common = {
        "wi": P["L0_attn_Wi"], "wq": P["L0_attn_Wq"], "wk": P["L0_attn_Wk"],
        "wv": P["L0_attn_Wv"], "wp": P["L0_attn_Wp"], "w2a": P["L0_attn_W2"],
        "w1a": P["L0_attn_W1"], "wh": P["L0_attn_Wh"], "wo": P["L0_attn_Wo"],
        "w1f": np.ascontiguousarray(P["L0_ffn_W1"].T),
        "w2f": np.ascontiguousarray(P["L0_ffn_W2"].T),
        "bi_t": t2(P["L0_attn_bi"]), "bq_t": t2(P["L0_attn_bq"]),
        "bk_t": t2(P["L0_attn_bk"]), "bv_t": t2(P["L0_attn_bv"]),
        "bp_t": t2(P["L0_attn_bp"]), "b2a_t": t2(P["L0_attn_b2"]),
        "b1a_t": t2(P["L0_attn_b1"]), "cb_t": t2(P["L0_attn_conv_b"]),
        "b1f_t": np.ascontiguousarray(P["L0_ffn_b1"].reshape(32, 128).T),
        "cw_t": cw_t,
        "bh_r": P["L0_attn_bh"].reshape(1, 3),
        "bo_r": P["L0_attn_bo"].reshape(1, D),
        "b2f_r": P["L0_ffn_b2"].reshape(1, D),
        "ident": np.eye(128, dtype=f32),
    }
    r = np.arange(128)[:, None]
    c = np.arange(192)[None, :]
    band = np.where((c >= r) & (c < r + 64), 0.0, NEG).astype(f32)
    blk0 = np.where((c >= r) & (c < r + 64) & (c >= 64), 0.0, NEG).astype(f32)

    in_maps = []
    for core in range(NCORE):
        start = TOWN * core
        first = core % 4 == 0
        xin = np.zeros((TH, D), f32)
        xin[HALO:] = x_full[start:start + TOWN]
        if not first:
            xin[0:HALO] = x_full[start - HALO:start]
        m = dict(common)
        m["xin"] = xin
        m["bandm"] = band
        m["blk0m"] = blk0 if first else band
        m["halo_scal"] = np.full((128, 1), 0.0 if first else 1.0, f32)
        m["tok0_scal"] = np.full((128, 1), 0.0 if first else 1.0, f32)
        in_maps.append(m)

    nc = _get_prog()
    res = run_bass_kernel_spmd(nc, in_maps, core_ids=list(range(NCORE)))
    out = np.concatenate([np.asarray(res.results[cc]["xout"])
                          for cc in range(NCORE)], axis=0)
    return out.reshape(2, 8192, D)


# version 11 (final)
# speedup vs baseline: 1.0435x; 1.0435x over previous
import math
from contextlib import ExitStack

import numpy as np

import concourse.bass as bass
import concourse.mybir as mybir
from concourse.bass_utils import run_bass_kernel_spmd

F32 = mybir.dt.float32
BF16 = mybir.dt.bfloat16
AF = mybir.ActivationFunctionType
ALU = mybir.AluOpType
AX = mybir.AxisListType

NEG = -1.0e30
NCORE = 8
TOWN = 2048
HALO = 128
TH = TOWN + HALO
NXT = TH // 128
D, DS, FF, LB = 1024, 256, 4096, 64
NKD = D // 128
NKF = FF // 128


class Ctr:
    __slots__ = ("h", "v")

    def __init__(self, nc, name):
        self.h = nc.alloc_semaphore(name)
        self.v = 0

    def inc(self, instr, n=1):
        instr.then_inc(self.h, n)
        self.v += n
        return self.v


def build_program():
    nc = bass.Bass("TRN2", target_bir_lowering=False,
                   detect_race_conditions=False)

    def din(name, shape, dt=F32):
        return nc.dram_tensor(name, shape, dt, kind="ExternalInput").ap()

    xin = din("xin", [TH, D])
    wi_d = din("wi", [D, DS])
    wq_d = din("wq", [DS, DS])
    wk_d = din("wk", [DS, DS])
    wv_d = din("wv", [DS, DS])
    wp_d = din("wp", [DS, DS])
    w2a_d = din("w2a", [DS, DS])
    w1a_d = din("w1a", [D, DS])
    wh_d = din("wh", [DS, 3])
    wo_d = din("wo", [DS, D])
    w1f_d = din("w1f", [D, FF])
    w2f_d = din("w2f", [FF, D])
    bias2_d = {}
    for nm in ("bi_t", "bq_t", "bk_t", "bv_t", "bp_t", "b2a_t", "b1a_t", "cb_t"):
        bias2_d[nm] = din(nm, [128, 2])
    b1f_d = din("b1f_t", [128, 32])
    cw_d = din("cw_t", [128, 10])
    bh_d = din("bh_r", [1, 3])
    bo_d = din("bo_r", [1, D])
    b2f_d = din("b2f_r", [1, D])
    ident_d = din("ident", [128, 128])
    bandm_d = din("bandm", [128, 192])
    blk0m_d = din("blk0m", [128, 192])
    hscal_d = din("halo_scal", [128, 1])
    t0scal_d = din("tok0_scal", [128, 1])

    import os as _os
    KDUMP = bool(_os.environ.get("KDUMP"))
    xout = nc.dram_tensor("xout", [TOWN, D], F32, kind="ExternalOutput").ap()
    if _os.environ.get("KFFN1"):
        d_st = nc.dram_tensor("d_st", [128, 32], F32, kind="ExternalOutput").ap()
        d_w1tl8 = nc.dram_tensor("d_w1tl8", [NKD, 128, FF], BF16, kind="ExternalOutput").ap()
        d_f1all = None
    if _os.environ.get("KF1"):
        d_f1all = nc.dram_tensor("d_f1all", [9 * NKF, 128, 256], F32, kind="ExternalOutput").ap()
    if KDUMP:
        d_x1 = nc.dram_tensor("d_x1", [TH, D], F32, kind="ExternalOutput").ap()
        d_h0 = nc.dram_tensor("d_h0", [2, 128, TH], F32, kind="ExternalOutput").ap()
        d_h1 = nc.dram_tensor("d_h1", [2, 128, 2112], F32, kind="ExternalOutput").ap()
        d_h2 = nc.dram_tensor("d_h2", [2, 128, TOWN], F32, kind="ExternalOutput").ap()
        d_x2 = nc.dram_tensor("d_x2", [TH, D], F32, kind="ExternalOutput").ap()

    scr_kind = {"kind": "ExternalOutput"} if KDUMP else {}
    w1scr = nc.dram_tensor("w1scr", [D, FF], BF16, **scr_kind).ap()
    w2scr = nc.dram_tensor("w2scr", [FF, D], BF16, **scr_kind).ap()
    if KDUMP:
        d_tern = nc.dram_tensor("d_tern", [128, 8], F32, kind="ExternalOutput").ap()
        d_w1tl = nc.dram_tensor("d_w1tl", [128, FF], BF16, kind="ExternalOutput").ap()
        d_w2T = nc.dram_tensor("d_w2T", [128, D], BF16, kind="ExternalOutput").ap()
        d_yhi = nc.dram_tensor("d_yhi", [128, 256], BF16, kind="ExternalOutput").ap()
        d_ylo = nc.dram_tensor("d_ylo", [128, 256], BF16, kind="ExternalOutput").ap()
        d_f1 = nc.dram_tensor("d_f1", [128, 256], F32, kind="ExternalOutput").ap()
        d_bst = nc.dram_tensor("d_bst", [128, 12], F32, kind="ExternalOutput").ap()
        d_stA = nc.dram_tensor("d_stA", [128, 34], F32, kind="ExternalOutput").ap()
        d_stC = nc.dram_tensor("d_stC", [128, 34], F32, kind="ExternalOutput").ap()

    def sal(name, shape, dt=F32):
        return nc.alloc_sbuf_tensor(name, shape, dt)

    x = [sal(f"x{j}", [128, D]) for j in range(NXT)]

    wq_sb = [sal(f"wq{k}", [128, DS]) for k in range(2)]
    wk_sb = [sal(f"wk{k}", [128, DS]) for k in range(2)]
    wv_sb = [sal(f"wv{k}", [128, DS]) for k in range(2)]
    wp_sb = [sal(f"wp{k}", [128, DS]) for k in range(2)]
    w2a_sb = [sal(f"w2a{k}", [128, DS]) for k in range(2)]

    wh_sb = [sal(f"wh{k}", [128, 3]) for k in range(2)]
    wo_sb = [sal(f"wo{k}", [128, D]) for k in range(2)]
    bias2 = {nm: sal(nm + "_s", [128, 2]) for nm in
             ("bi_t", "bq_t", "bk_t", "bv_t", "bp_t", "b2a_t", "b1a_t", "cb_t")}
    b1f_sb = sal("b1f_s", [128, 32])
    cw_sb = sal("cw_s", [128, 10])
    bh_sb = sal("bh_s", [1, 3])
    bo_sb = sal("bo_s", [1, D])
    b2f_sb = sal("b2f_s", [1, D])
    ident = sal("ident_s", [128, 128])
    bandm = sal("bandm_s", [128, 192])
    blk0m = sal("blk0m_s", [128, 192])
    hscal = sal("hscal_s", [128, 1])
    t0scal = sal("t0scal_s", [128, 1])
    ones_r = sal("ones_r", [1, 128])
    zrow16 = sal("zrow16", [1, 8], BF16)
    ones_c = sal("ones_c", [128, 1])
    eps_t = sal("eps_t", [128, 1])
    sc16 = sal("sc16", [128, 1])

    nmu1 = sal("nmu1", [128, NXT]); rsd1 = sal("rsd1", [128, NXT])
    nmu2 = sal("nmu2", [128, NXT]); rsd2 = sal("rsd2", [128, NXT])
    nmu3 = sal("nmu3", [128, 16]);  rsd3 = sal("rsd3", [128, 16])
    a2t = sal("a2t", [128, NXT]); b2ct = sal("b2ct", [128, NXT])
    var_all = sal("var_all", [128, NXT])
    sd_all = sal("sd_all", [128, NXT])
    bst = sal("bst", [128, 12])
    if KDUMP:
        bst12 = sal("bst12", [128, 12])
        d_stA_sb = sal("d_stA_sb", [128, 34])
    bagg = sal("bagg", [128, 2])
    rtmp = sal("rtmp", [128, 32])
    racc = sal("racc", [128, 1])
    tot1 = sal("tot1", [1, 1])
    half1 = sal("half1", [128, 1]); nhalf1 = sal("nhalf1", [128, 1])
    half2 = sal("half2", [128, 1]); nhalf2 = sal("nhalf2", [128, 1])

    p = [nc.alloc_psum_tensor(f"p{i}", [128, 512], F32) for i in range(8)]

    sx = Ctr(nc, "sx")
    swa = Ctr(nc, "swa")
    swr = Ctr(nc, "swr")
    sscr = Ctr(nc, "sscr")
    swt = Ctr(nc, "swt")
    sout = Ctr(nc, "sout")
    cv = Ctr(nc, "cv")
    cs = Ctr(nc, "cs")
    ct = Ctr(nc, "ct")
    cg = Ctr(nc, "cg")

    with nc.Block() as _blk:
        V, S, T, G, SY = nc.vector, nc.scalar, nc.tensor, nc.gpsimd, nc.sync


        def tg(ap, n=1):
            return ct.inc(T.matmul(ap, zrow16[0:1, 0:1], zrow16[0:1, 0:1],
                                   start=False, stop=True,
                                   skip_group_check=True), n)

        def barrier(engines):
            for e in engines:
                for c in (cv, cs, ct, cg):
                    e.wait_ge(c.h, c.v)

        for k in range(2):
            r = slice(128 * k, 128 * (k + 1))
            swa.inc(G.dma_start(out=wq_sb[k][:], in_=wq_d[r, :]), 16)
            swa.inc(G.dma_start(out=wk_sb[k][:], in_=wk_d[r, :]), 16)
            swa.inc(G.dma_start(out=wv_sb[k][:], in_=wv_d[r, :]), 16)
            swa.inc(G.dma_start(out=wp_sb[k][:], in_=wp_d[r, :]), 16)
            swa.inc(G.dma_start(out=w2a_sb[k][:], in_=w2a_d[r, :]), 16)
            swa.inc(G.dma_start(out=wh_sb[k][:], in_=wh_d[r, :]), 16)
            swa.inc(G.dma_start(out=wo_sb[k][:], in_=wo_d[r, :]), 16)
        for nm, dd in bias2_d.items():
            swa.inc(G.dma_start(out=bias2[nm][:], in_=dd), 16)
        for dst, src in ((b1f_sb, b1f_d), (cw_sb, cw_d), (bh_sb, bh_d),
                         (bo_sb, bo_d), (b2f_sb, b2f_d), (ident, ident_d),
                         (bandm, bandm_d), (blk0m, blk0m_d), (hscal, hscal_d),
                         (t0scal, t0scal_d)):
            swa.inc(G.dma_start(out=dst[:], in_=src), 16)
        swa_done = swa.v

        V.memset(ones_r[:], 1.0)
        V.memset(zrow16[:], 0.0)
        V.memset(ones_c[:], 1.0)
        V.memset(eps_t[:], 1e-5)
        cv.inc(V.memset(sc16[:], 1.0 / 16.0))
        consts_ready = cv.v
        T.wait_ge(cv.h, consts_ready)
        T.matmul(p[5][0:1, 0:1], zrow16[0:1, 0:1], zrow16[0:1, 0:1],
                 start=True, stop=True, skip_group_check=True)

        def emit_stats_tile(jj, j, nmu_t):
            g0 = cv.inc(V.bn_stats(bst[:, 0:6], x[j][:, 0:512]))
            V.wait_ge(cv.h, g0)
            g1 = cv.inc(V.bn_stats(bst[:, 6:12], x[j][:, 512:1024]))
            V.wait_ge(cv.h, g1)
            g2 = cv.inc(V.bn_aggr(bagg[:], bst[:]))
            V.wait_ge(cv.h, g2)
            V.tensor_scalar_mul(nmu_t[:, jj:jj + 1], bagg[:, 0:1], -1.0)
            return V.tensor_copy(out=var_all[:, jj:jj + 1], in_=bagg[:, 1:2])

        def emit_stats_fin(n, rsd_t, last_cp):
            g3 = cv.inc(last_cp)
            S.wait_ge(cv.h, g3)
            s1 = cs.inc(S.activation(out=sd_all[:, 0:n], in_=var_all[:, 0:n],
                                     func=AF.Sqrt, bias=eps_t[:, 0:1], scale=1.0))
            V.wait_ge(cs.h, s1)
            g4 = cv.inc(V.reciprocal(rsd_t[:, 0:n], sd_all[:, 0:n]))
            V.wait_ge(cv.h, g4)
            return cv.v

        def emit_stats(tiles, nmu_t, rsd_t, wait_fn):
            for jj, j in enumerate(tiles):
                wait_fn(jj, j)
                last_cp = emit_stats_tile(jj, j, nmu_t)
            return emit_stats_fin(len(tiles), rsd_t, last_cp)

        def wait_x(jj, j):
            V.wait_ge(sx.h, 32 * NXT)

        with ExitStack() as es:
            w1raw = [es.enter_context(nc.sbuf_tensor(f"w1raw{i}", [128, 1024], F32))
                     for i in range(7)]
            w2raw = [es.enter_context(nc.sbuf_tensor(f"w2raw{i}", [128, 1024], F32))
                     for i in range(7)]
            tmpa = [es.enter_context(nc.sbuf_tensor(f"tmpa{i}", [128, 1024], F32))
                    for i in range(2)]
            tmpb1 = [es.enter_context(nc.sbuf_tensor(f"tmpb1_{i}", [128, 1024], F32))
                     for i in range(2)]
            tmpb2 = [es.enter_context(nc.sbuf_tensor(f"tmpb2_{i}", [128, 1024], F32))
                     for i in range(2)]
            t16a = [es.enter_context(nc.sbuf_tensor(f"t16a{i}", [128, 1024], BF16))
                    for i in range(3)]
            t16b = [es.enter_context(nc.sbuf_tensor(f"t16b{i}", [128, 1024], BF16))
                    for i in range(3)]
            rtmp2 = es.enter_context(nc.sbuf_tensor("rtmp2", [128, 32], F32))
            racc2 = es.enter_context(nc.sbuf_tensor("racc2", [128, 1], F32))
            tot2 = es.enter_context(nc.sbuf_tensor("tot2", [1, 1], F32))

            w1slices = [(r, s) for r in range(NKD) for s in range(4)]
            nW = len(w1slices)
            PF = 6
            w1A_dv = []
            w2A_dv = []
            w1A_red_done = []
            w2A_red_done = []

            def issue_w1(i, dvl):
                r, s = w1slices[i]
                for q4 in range(2):
                    cq = slice(512 * q4, 512 * (q4 + 1))
                    last = swr.inc(SY.dma_start(
                        out=w1raw[i % 7][:, cq],
                        in_=w1f_d[128 * r:128 * (r + 1),
                                  1024 * s + 512 * q4:1024 * s + 512 * (q4 + 1)]),
                        16)
                dvl.append(last)

            def issue_w2(i, dvl):
                for q4 in range(2):
                    cq = slice(512 * q4, 512 * (q4 + 1))
                    last = swr.inc(SY.dma_start(
                        out=w2raw[i % 7][:, cq],
                        in_=w2f_d[128 * i:128 * (i + 1), cq]), 16)
                dvl.append(last)

            for i in range(PF):
                issue_w1(i, w1A_dv)
                issue_w2(i, w2A_dv)
            for j in range(nW):
                if j < NXT:
                    r_ = slice(128 * j, 128 * (j + 1))
                    sx.inc(SY.dma_start(out=x[j][:, 0:512],
                                        in_=xin[r_, 0:512]), 16)
                    sx.inc(SY.dma_start(out=x[j][:, 512:1024],
                                        in_=xin[r_, 512:1024]), 16)
                i = j + PF
                if i < nW:
                    if i >= 7:
                        SY.wait_ge(cs.h, w1A_red_done[i - 7])
                        SY.wait_ge(cv.h, w2A_red_done[i - 7])
                    issue_w1(i, w1A_dv)
                    issue_w2(i, w2A_dv)
                S.wait_ge(swr.h, w1A_dv[min(j + 3, nW - 1)])
                if j == nW - 1:
                    S.wait_ge(sx.h, 32)
                    d0 = cs.inc(S.activation(out=tmpa[0][:], in_=x[0][:],
                                             func=AF.Abs, bias=0.0, scale=1.0))
                    S.wait_ge(cs.h, d0)
                w1A_red_done.append(cs.inc(S.activation(
                    out=tmpa[j % 2][:], in_=w1raw[j % 7][:], func=AF.Abs,
                    bias=0.0, scale=1.0, accum_out=rtmp[:, j:j + 1])))
                S.wait_ge(cs.h, w1A_red_done[-1])
                V.wait_ge(swr.h, w2A_dv[min(j + 3, nW - 1)])
                if j == nW - 1:
                    V.wait_ge(sx.h, 32)
                    d0 = cv.inc(V.tensor_copy(out=tmpb1[0][:], in_=x[0][:]))
                    V.wait_ge(cv.h, d0)
                w2A_red_done.append(cv.inc(V.reduce_sum(
                    rtmp2[:, j:j + 1], w2raw[j % 7][:], axis=AX.X,
                    apply_absolute_value=True)))
                V.wait_ge(cv.h, w2A_red_done[-1])
            V.wait_ge(cs.h, w1A_red_done[-1])
            g = cv.inc(V.reduce_sum(racc[:], rtmp[:], axis=AX.X))
            V.wait_ge(cv.h, g)
            w1A_red = cv.v
            g = cv.inc(V.reduce_sum(racc2[:], rtmp2[:], axis=AX.X))
            V.wait_ge(cv.h, g)
            w2A_red = cv.v

            T.wait_ge(cv.h, w1A_red)
            T.wait_ge(cv.h, w2A_red)
            T.matmul(p[0][0:1, 0:1], racc[:], ones_c[:], start=True, stop=True)
            T.matmul(p[1][0:1, 0:1], racc2[:], ones_c[:], start=True, stop=True)
            t1 = tg(p[5][0:1, 0:1])
            V.wait_ge(ct.h, t1)
            V.tensor_scalar(out=tot1[:], in0=p[0][0:1, 0:1],
                            scalar1=1.0 / (FF * D), scalar2=1e-8,
                            op0=ALU.mult, op1=ALU.max)
            g = cv.inc(V.tensor_scalar(out=tot2[:], in0=p[1][0:1, 0:1],
                                       scalar1=1.0 / (FF * D), scalar2=1e-8,
                                       op0=ALU.mult, op1=ALU.max))
            T.wait_ge(cv.h, g)
            T.matmul(p[0][0:128, 0:1], ones_r[:], tot1[:], start=True, stop=True)
            T.matmul(p[1][0:128, 0:1], ones_r[:], tot2[:], start=True, stop=True)
            t2 = tg(p[5][0:1, 0:1])
            V.wait_ge(ct.h, t2)
            V.tensor_scalar_mul(half1[:], p[0][0:128, 0:1], 0.5)
            V.tensor_scalar_mul(nhalf1[:], p[0][0:128, 0:1], -0.5)
            V.tensor_scalar_mul(half2[:], p[1][0:128, 0:1], 0.5)
            g = cv.inc(V.tensor_scalar_mul(nhalf2[:], p[1][0:128, 0:1], -0.5))
            V.wait_ge(cv.h, g)
            G.wait_ge(cv.h, g)
            scal_ready_v = cv.v

            w1t_done = []
            w1scr_dma = []
            w1B_dv = []
            w2t_done = []
            w2scr_dma = []
            w2B_dv = []
            SY.wait_ge(cs.h, w1A_red_done[-1])
            SY.wait_ge(cv.h, w2A_red_done[-1])
            for i in range(PF):
                issue_w1(i, w1B_dv)
                issue_w2(i, w2B_dv)
            for j in range(nW):
                i = j + PF
                if i < nW:
                    if i >= 7:
                        SY.wait_ge(cv.h, w1t_done[i - 7])
                        SY.wait_ge(cv.h, w2t_done[i - 7])
                    issue_w1(i, w1B_dv)
                    issue_w2(i, w2B_dv)
                rj, sj = w1slices[j]
                V.wait_ge(swr.h, w1B_dv[min(j + 3, nW - 1)])
                if j == nW - 1:
                    V.wait_ge(sx.h, 32)
                    d0 = cv.inc(V.tensor_copy(out=tmpb1[0][:], in_=x[0][:]))
                    V.wait_ge(cv.h, d0)
                if j >= 3:
                    V.wait_ge(sscr.h, w1scr_dma[j - 3])
                V.tensor_single_scalar(tmpb1[j % 2][:], w1raw[j % 7][:],
                                       nhalf1[:], ALU.is_le)
                w1t_done.append(cv.inc(V.scalar_tensor_tensor(
                    out=t16a[j % 3][:], in0=w1raw[j % 7][:], scalar=half1[:],
                    in1=tmpb1[j % 2][:], op0=ALU.is_ge, op1=ALU.subtract)))
                G.wait_ge(cv.h, w1t_done[j])
                w1scr_dma.append(sscr.inc(G.dma_start(
                    out=w1scr[128 * rj:128 * (rj + 1), 1024 * sj:1024 * (sj + 1)],
                    in_=t16a[j % 3][:]), 16))
                V.wait_ge(swr.h, w2B_dv[min(j + 3, nW - 1)])
                if j == nW - 1:
                    V.wait_ge(sx.h, 32)
                    d0 = cv.inc(V.tensor_copy(out=tmpb2[0][:], in_=x[0][:]))
                    V.wait_ge(cv.h, d0)
                if j >= 3:
                    V.wait_ge(sscr.h, w2scr_dma[j - 3])
                V.tensor_single_scalar(tmpb2[j % 2][:], w2raw[j % 7][:],
                                       nhalf2[:], ALU.is_le)
                w2t_done.append(cv.inc(V.scalar_tensor_tensor(
                    out=t16b[j % 3][:], in0=w2raw[j % 7][:], scalar=half2[:],
                    in1=tmpb2[j % 2][:], op0=ALU.is_ge, op1=ALU.subtract)))
                G.wait_ge(cv.h, w2t_done[j])
                w2scr_dma.append(sscr.inc(G.dma_start(
                    out=w2scr[128 * j:128 * (j + 1), :],
                    in_=t16b[j % 3][:]), 16))
            w1scr_all = w1scr_dma[-1]
            w2scr_all = w2scr_dma[-1]
            tern_done = (cv.v, cg.v, sscr.v)

        stats1_ready = emit_stats(list(range(NXT)), nmu1, rsd1, wait_x)


        if KDUMP:
            V.tensor_copy(out=d_stA_sb[:, 0:17], in_=nmu1[:])
            gC = cv.inc(V.tensor_copy(out=d_stA_sb[:, 17:34], in_=rsd1[:]))
            SY.wait_ge(cv.h, gC)
            sout.inc(SY.dma_start(out=d_stC, in_=d_stA_sb[:]), 16)
        if KDUMP:
            V.wait_ge(cv.h, scal_ready_v)
            V.tensor_copy(out=rtmp[:, 0:1], in_=half1[:])
            V.tensor_copy(out=rtmp[:, 1:2], in_=nhalf1[:])
            V.tensor_copy(out=rtmp[:, 2:3], in_=half2[:])
            V.tensor_copy(out=rtmp[:, 3:4], in_=nhalf2[:])
            V.wait_ge(cv.h, stats1_ready)
            V.tensor_copy(out=rtmp[:, 4:5], in_=nmu1[:, 0:1])
            V.tensor_copy(out=rtmp[:, 5:6], in_=rsd1[:, 0:1])
            V.tensor_copy(out=rtmp[:, 6:7], in_=nmu1[:, 16:17])
            gdv = cv.inc(V.tensor_copy(out=rtmp[:, 7:8], in_=rsd1[:, 16:17]))
            SY.wait_ge(cv.h, gdv)
            sout.inc(SY.dma_start(out=d_tern, in_=rtmp[:, 0:8]), 16)
        import os as _os
        if _os.environ.get("KDEBUG"):
            print("sbuf before ffn:", nc.sbuf_bytes_remaining)
        KSER = _os.environ.get("KSER", "")
        w1tl_stash = []
        def emit_ffn(tiles, nmu_t, rsd_t, stats_ready, xtile_wait, is_ffn2,
                     stats_cb=None):
            groups = []
            i = 0
            while i < len(tiles):
                groups.append(tiles[i:i + 2])
                i += 2
            with ExitStack() as es:
                w1tl = [es.enter_context(nc.sbuf_tensor(f"w1tl{i}_{is_ffn2}", [128, FF], BF16))
                        for i in range(NKD)]
                w1tl_stash.clear(); w1tl_stash.extend(w1tl)
                w2T = [es.enter_context(nc.sbuf_tensor(f"w2T{i}_{is_ffn2}", [128, D], BF16))
                       for i in range(5)]
                ybuf = [es.enter_context(nc.sbuf_tensor(f"ybuf{i}_{is_ffn2}", [128, D], F32))
                        for i in range(1)]
                yhi = [[es.enter_context(nc.sbuf_tensor(f"yhi{s}_{i}_{is_ffn2}", [128, 256], BF16))
                        for i in range(NKD)] for s in range(2)]
                ylo = [[es.enter_context(nc.sbuf_tensor(f"ylo{s}_{i}_{is_ffn2}", [128, 256], BF16))
                        for i in range(NKD)] for s in range(2)]
                f1f = [es.enter_context(nc.sbuf_tensor(f"f1f{i}_{is_ffn2}", [128, 256], F32))
                       for i in range(2)]
                f1h = [es.enter_context(nc.sbuf_tensor(f"f1h{i}_{is_ffn2}", [128, 256], BF16))
                       for i in range(2)]
                f1l = [es.enter_context(nc.sbuf_tensor(f"f1l{i}_{is_ffn2}", [128, 256], BF16))
                       for i in range(2)]

                SY.wait_ge(sscr.h, w2scr_all)
                V.wait_ge(sscr.h, w2scr_all)
                S.wait_ge(sscr.h, w2scr_all)
                w1tl_dma = []
                for k in range(NKD):
                    for q_ in range(4):
                        cq = slice(1024 * q_, 1024 * (q_ + 1))
                        w1tl_dma.append(swt.inc(SY.dma_start(
                            out=w1tl[k][:, cq],
                            in_=w1scr[128 * k:128 * (k + 1), cq]), 16))

                stats_pending = []
                w2T_load = {}
                w2T_free = {}
                mm1_free = {}
                split_free = {}
                ybuf_free = {}
                tr_free = {}
                out2_free = {}
                ev_done = {}
                fidx = 0

                split_ready = {}

                def emit_prep_tile(g, lt):
                    txs_g = groups[g]
                    stg = g % 2
                    j = txs_g[lt]
                    jj = tiles.index(j)
                    V.wait_ge(cv.h, stats_ready)
                    if stg in split_free:
                        S.wait_ge(ct.h, split_free[stg])
                        V.wait_ge(ct.h, split_free[stg])
                    xtile_wait(V, j)
                    if ybuf_free:
                        V.wait_ge(ct.h, max(ybuf_free.values()))
                    yv = cv.inc(V.tensor_scalar(
                        out=ybuf[0][:], in0=x[j][:],
                        scalar1=nmu_t[:, jj:jj + 1],
                        scalar2=rsd_t[:, jj:jj + 1],
                        op0=ALU.add, op1=ALU.mult))
                    for w in (0, 1):
                        bank = 6 + w
                        T.wait_ge(cv.h, yv)
                        if bank in tr_free:
                            T.wait_ge(cv.h, tr_free[bank])
                        for c in range(4 * w, 4 * w + 4):
                            sl = (c % 4) * 128
                            last = T.transpose(
                                p[bank][0:128, sl:sl + 128],
                                ybuf[0][:, 128 * c:128 * (c + 1)], ident[:])
                        tv = ct.inc(last)
                        ybuf_free[(g, lt)] = tv
                        S.wait_ge(ct.h, tv)
                        for c in range(4 * w, 4 * w + 4):
                            sl = (c % 4) * 128
                            last = S.activation(
                                out=yhi[stg][c][:, 128 * lt:128 * lt + 128],
                                in_=p[bank][0:128, sl:sl + 128],
                                func=AF.Identity, bias=0.0, scale=1.0)
                        sv = cs.inc(last)
                        V.wait_ge(cs.h, sv)
                        for c in range(4 * w, 4 * w + 4):
                            sl = (c % 4) * 128
                            last = V.tensor_sub(
                                ylo[stg][c][:, 128 * lt:128 * lt + 128],
                                p[bank][0:128, sl:sl + 128],
                                yhi[stg][c][:, 128 * lt:128 * lt + 128])
                        tr_free[bank] = cv.inc(last)
                    if lt == len(txs_g) - 1:
                        split_ready[stg] = (cs.v, cv.v)

                for lt_ in range(len(groups[0])):
                    emit_prep_tile(0, lt_)
                if KDUMP and not is_ffn2:
                    SY.wait_ge(cs.h, split_ready[0][0])
                    SY.wait_ge(cv.h, split_ready[0][1])
                    SY.wait_ge(swt.h, w1tl_dma[-1])
                    sout.inc(SY.dma_start(out=d_w1tl, in_=w1tl[0][:]), 16)
                    sout.inc(SY.dma_start(out=d_yhi, in_=yhi[0][0][:]), 16)
                    sout.inc(SY.dma_start(out=d_ylo, in_=ylo[0][0][:]), 16)

                for gi, txs in enumerate(groups):
                    NT = 128 * len(txs)
                    st = gi % 2
                    lov_of = {}

                    ngq = 32 * len(groups)

                    def emit_mm2(fq, gq):
                        T.wait_ge(cv.h, lov_of[fq])
                        T.wait_ge(swt.h, w2T_load[min(gq + 2, ngq - 1)])
                        if fq == 0:
                            for b in range(2 * len(txs)):
                                if b in out2_free:
                                    T.wait_ge(cv.h, out2_free[b])
                        fbq = fq % 2
                        for lt in range(len(txs)):
                            for sidx, srcq in enumerate((f1h, f1l)):
                                for half in range(2):
                                    lastq = T.matmul(
                                        p[2 * lt + half][0:128, 0:512],
                                        srcq[fbq][:, 128 * lt:128 * lt + 128],
                                        w2T[gq % 5][:, 512 * half:512 * (half + 1)],
                                        start=(fq == 0 and sidx == 0),
                                        stop=False)
                        w2T_free[gq] = ct.inc(lastq)
                        nq = gq + 5
                        if nq < ngq:
                            fq3 = nq % NKF
                            SY.wait_ge(ct.h, w2T_free[gq])
                            w2T_load[nq] = swt.inc(SY.dma_start(
                                out=w2T[nq % 5][:],
                                in_=w2scr[128 * fq3:128 * (fq3 + 1), :]), 16)

                    for f in range(NKF):
                        if fidx == 0:
                            for q_ in range(5):
                                w2T_load[q_] = swt.inc(SY.dma_start(
                                    out=w2T[q_ % 5][:],
                                    in_=w2scr[128 * q_:128 * (q_ + 1), :]), 16)
                        bank = 4 + (f % 2)
                        if f == 0:
                            T.wait_ge(swt.h, w1tl_dma[-1])
                            T.wait_ge(cs.h, split_ready[st][0])
                            T.wait_ge(cv.h, split_ready[st][1])
                        if bank in mm1_free:
                            T.wait_ge(cs.h, mm1_free[bank])
                        nmm = 0
                        for kk in range(NKD):
                            for src in (yhi, ylo):
                                last = T.matmul(
                                    p[bank][0:128, 0:NT],
                                    w1tl[kk][:, 128 * f:128 * (f + 1)],
                                    src[st][kk][:, 0:NT],
                                    start=(nmm == 0), stop=(nmm == 2 * NKD - 1))
                                nmm += 1
                        mm1v = ct.inc(last)
                        if f == NKF - 1:
                            split_free[st] = mm1v
                        fb = f % 2
                        S.wait_ge(ct.h, mm1v)
                        S.activation(out=f1f[fb][:, 0:NT], in_=p[bank][0:128, 0:NT],
                                     func=AF.Gelu, bias=b1f_sb[:, f:f + 1], scale=1.0)
                        mm1_free[bank] = cs.inc(S.activation(
                            out=f1h[fb][:, 0:NT], in_=f1f[fb][:, 0:NT],
                            func=AF.Identity, bias=0.0, scale=1.0))
                        V.wait_ge(cs.h, mm1_free[bank])
                        lov_of[f] = cv.inc(V.tensor_sub(f1l[fb][:, 0:NT],
                                                        f1f[fb][:, 0:NT],
                                                        f1h[fb][:, 0:NT]))
                        if f == 8 and gi + 1 < len(groups):
                            emit_prep_tile(gi + 1, 0)
                        if f == 20 and gi + 1 < len(groups) \
                                and len(groups[gi + 1]) > 1:
                            emit_prep_tile(gi + 1, 1)
                        if f in (14, 26) and stats_pending:
                            stats_cb(stats_pending.pop(0))
                        if f >= 1:
                            emit_mm2(f - 1, fidx - 1)
                        fidx += 1
                    emit_mm2(NKF - 1, fidx - 1)
                    for lt in range(len(txs)):
                        for half in range(2):
                            last = T.matmul(
                                p[2 * lt + half][0:128, 0:512],
                                ones_r[0:1, 0:128],
                                b2f_sb[0:1, 512 * half:512 * (half + 1)],
                                start=False, stop=True)
                    bv_ = ct.inc(last)
                    if "c" in KSER:
                        barrier([V, S, T, G, SY])
                    V.wait_ge(ct.h, bv_)
                    for lt, j in enumerate(txs):
                        for half in range(2):
                            hc = slice(512 * half, 512 * (half + 1))
                            last = V.scalar_tensor_tensor(
                                out=x[j][:, hc], in0=p[2 * lt + half][0:128, 0:512],
                                scalar=0.5, in1=x[j][:, hc],
                                op0=ALU.mult, op1=ALU.add)
                            out2_free[2 * lt + half] = cv.inc(last)
                        ev_done[j] = cv.v
                        if stats_cb is not None:
                            stats_pending.append(j)
                        if is_ffn2:
                            SY.wait_ge(cv.h, ev_done[j])
                            sout.inc(SY.dma_start(
                                out=xout[128 * (j - 1):128 * j, 0:512],
                                in_=x[j][:, 0:512]), 16)
                            sout.inc(SY.dma_start(
                                out=xout[128 * (j - 1):128 * j, 512:1024],
                                in_=x[j][:, 512:1024]), 16)
                while stats_pending:
                    stats_cb(stats_pending.pop(0))
                return ev_done

        def wait_none(V_, j):
            pass
        for e_ in (T, S, V, G):
            e_.wait_ge(swa.h, swa_done)
        st2_cp = {}

        def cb_stats2(j):
            st2_cp["cp"] = emit_stats_tile(j, j, nmu2)
        ffn1_ev = emit_ffn(list(range(NXT)), nmu1, rsd1, stats1_ready,
                           wait_none, False, stats_cb=cb_stats2)

        KFFN1 = bool(_os.environ.get("KFFN1"))
        if KFFN1:
            for k in range(NKD):
                sout.inc(SY.dma_start(out=d_w1tl8[k], in_=w1tl_stash[k][:]), 16)
            V.wait_ge(cv.h, stats1_ready)
            V.tensor_copy(out=rtmp[:, 0:17], in_=nmu1[:])
            gS = cv.inc(V.tensor_copy(out=rtmp[:, 17:32], in_=rsd1[:, 0:15]))
            SY.wait_ge(cv.h, gS)
            sout.inc(SY.dma_start(out=d_st[0:128, 0:32], in_=rtmp[:]), 16)
            for j in range(1, NXT):
                SY.wait_ge(cv.h, ffn1_ev[j])
                sout.inc(SY.dma_start(out=xout[128 * (j - 1):128 * j, :],
                                      in_=x[j][:]), 16)
        if not KFFN1:
            barrier([V, S, T, G, SY])
            if KDUMP:
                for j in range(NXT):
                    sout.inc(SY.dma_start(out=d_x1[128 * j:128 * (j + 1), :],
                                          in_=x[j][:]), 16)
            stats2_ready = emit_stats_fin(NXT, rsd2, st2_cp["cp"])
            V.tensor_scalar_add(a2t[:], rsd2[:], 1.0)
            g = cv.inc(V.tensor_mul(b2ct[:], nmu2[:], rsd2[:]))
            V.wait_ge(cv.h, g)
            zadd_ready = cv.v
            st3_cp = {}

            with ExitStack() as es:
                wi_sb = [es.enter_context(nc.sbuf_tensor(f"wi{k}", [128, DS], F32))
                         for k in range(NKD)]
                w1a_sb = [es.enter_context(nc.sbuf_tensor(f"w1a{k}", [128, DS], F32))
                          for k in range(NKD)]
                for k in range(NKD):
                    swa.inc(G.dma_start(out=wi_sb[k][:],
                                        in_=wi_d[128 * k:128 * (k + 1), :]), 16)
                    swa.inc(G.dma_start(out=w1a_sb[k][:],
                                        in_=w1a_d[128 * k:128 * (k + 1), :]), 16)
                T.wait_ge(swa.h, swa.v)
                hA = [es.enter_context(nc.sbuf_tensor(f"hA{i}", [128, TH], F32))
                      for i in range(2)]
                hB = [es.enter_context(nc.sbuf_tensor(f"hB{i}", [128, 2112], F32))
                      for i in range(2)]
                zbuf = es.enter_context(nc.sbuf_tensor("zbuf", [128, D], F32))
                zT0 = [es.enter_context(nc.sbuf_tensor(f"zT{i}", [128, 128], F32))
                       for i in range(NKD)]
                k_g = [es.enter_context(nc.sbuf_tensor(f"k_g{i}", [128, 576], F32))
                       for i in range(2)]
                v_g = [es.enter_context(nc.sbuf_tensor(f"v_g{i}", [64, 256], F32))
                       for i in range(9)]
                q_g = [es.enter_context(nc.sbuf_tensor(f"q_g{i}", [128, 512], F32))
                       for i in range(2)]
                mx_g = [es.enter_context(nc.sbuf_tensor(f"mx_g{i}", [128, 512], F32))
                        for i in range(2)]
                sm_g = [es.enter_context(nc.sbuf_tensor(f"sm_g{i}", [128, 512], F32))
                        for i in range(2)]
                th_g = [es.enter_context(nc.sbuf_tensor(f"th_g{i}", [128, 512], F32))
                        for i in range(2)]
                d1_g = [es.enter_context(nc.sbuf_tensor(f"d1_g{i}", [128, 512], F32))
                        for i in range(2)]
                sc_sb = [es.enter_context(nc.sbuf_tensor(f"sc_sb{i}", [128, 192], F32))
                         for i in range(2)]
                ef_sb = [es.enter_context(nc.sbuf_tensor(f"ef_sb{i}", [128, 192], F32))
                         for i in range(2)]
                mk_sb = [es.enter_context(nc.sbuf_tensor(f"mk_sb{i}", [128, 192], F32))
                         for i in range(2)]
                top8 = [es.enter_context(nc.sbuf_tensor(f"top8_{i}", [128, 8], F32))
                        for i in range(2)]
                negm = [es.enter_context(nc.sbuf_tensor(f"negm{i}", [128, 1], F32))
                        for i in range(2)]
                den = [es.enter_context(nc.sbuf_tensor(f"den{i}", [128, 1], F32))
                       for i in range(2)]
                rd_ = [es.enter_context(nc.sbuf_tensor(f"rd_{i}", [128, 1], F32))
                       for i in range(2)]
                wT64 = [[es.enter_context(nc.sbuf_tensor(f"wT64_{s}_{i}", [64, 128], F32))
                         for i in range(3)] for s in range(2)]
                ph_sb = [es.enter_context(nc.sbuf_tensor(f"ph_sb{i}", [128, 136], F32))
                         for i in range(2)]
                pb_sb = [es.enter_context(nc.sbuf_tensor(f"pb_sb{i}", [128, 136], F32))
                         for i in range(2)]
                gate_row = es.enter_context(
                    nc.sbuf_tensor("gate_row", [1, 2112], F32))

                zT = [[t[:] for t in zT0],
                      [th_g[c // 4][:, 128 * (c % 4):128 * (c % 4) + 128]
                       for c in range(NKD)]]
                h0_mark = {}

                def emit_h0_mm(jq):
                    T.wait_ge(cs.h, h0_mark[(jq, "ev")])
                    if jq >= 1:
                        T.wait_ge(cs.h, h0_mark[(jq - 1, "h")])
                    for m in range(2):
                        for kk in range(NKD):
                            last = T.matmul(p[m][0:128, 0:128],
                                            wi_sb[kk][:, 128 * m:128 * (m + 1)],
                                            zT[jq % 2][kk], start=(kk == 0),
                                            stop=(kk == NKD - 1))
                    mmv = tg(p[5][0:1, 0:1])
                    h0_mark[(jq, "mm")] = mmv
                    S.wait_ge(ct.h, mmv)
                    for m in range(2):
                        last = S.activation(out=hA[m][:, 128 * jq:128 * (jq + 1)],
                                            in_=p[m][0:128, 0:128],
                                            func=AF.Identity,
                                            bias=bias2["bi_t"][:, m:m + 1], scale=1.0)
                    h0_mark[(jq, "h")] = cs.inc(last)

                for j in range(NXT):
                    V.wait_ge(cv.h, stats2_ready)
                    if j >= 1:
                        V.wait_ge(ct.h, h0_mark[(j - 1, "tr")])
                    zv = cv.inc(V.tensor_scalar(
                        out=zbuf[:], in0=x[j][:], scalar1=nmu2[:, j:j + 1],
                        scalar2=rsd2[:, j:j + 1], op0=ALU.add, op1=ALU.mult))
                    T.wait_ge(cv.h, zv)
                    if j >= 1:
                        T.wait_ge(cs.h, h0_mark[(j - 1, "ev")])
                    for c in range(NKD):
                        bank = 6 + (c // 4)
                        sl = (c % 4) * 128
                        last = T.transpose(p[bank][0:128, sl:sl + 128],
                                           zbuf[:, 128 * c:128 * (c + 1)], ident[:])
                    h0_mark[(j, "tr")] = tg(p[5][0:1, 0:1])
                    S.wait_ge(ct.h, h0_mark[(j, "tr")])
                    if j >= 2:
                        S.wait_ge(ct.h, h0_mark[(j - 2, "mm")])
                    for c in range(NKD):
                        bank = 6 + (c // 4)
                        sl = (c % 4) * 128
                        last = S.copy(out=zT[j % 2][c],
                                      in_=p[bank][0:128, sl:sl + 128])
                    h0_mark[(j, "ev")] = cs.inc(last)
                    if j >= 1:
                        emit_h0_mm(j - 1)
                emit_h0_mm(NXT - 1)
                V.wait_ge(cs.h, h0_mark[(NXT - 1, "h")])
                V.tensor_single_scalar(hA[0][:, 0:128], hA[0][:, 0:128], hscal[:], ALU.mult)
                g = cv.inc(V.tensor_single_scalar(hA[1][:, 0:128], hA[1][:, 0:128],
                                                  hscal[:], ALU.mult))
                h0_ready = (cs.v, cv.v)

                def emit_attn_step(IN, OUT, Sn, blocks, blk0_ob, tok0_col, in_ready):
                    Tq = Sn - 64
                    nch = Sn // 16
                    T.wait_ge(cs.h, in_ready[0])
                    T.wait_ge(cv.h, in_ready[1])
                    V.wait_ge(cs.h, in_ready[0])
                    S.wait_ge(cv.h, in_ready[1])
                    for c in range(2):
                        x3 = IN[c][:, 0:Sn].rearrange("p (a b) -> p a b", b=16)
                        last = V.reduce_sum(ph_sb[c][:, 0:nch], x3, axis=AX.X)
                    g = cv.inc(last)
                    V.wait_ge(cv.h, g)
                    T.wait_ge(cv.h, g)
                    for m in range(2):
                        for kk in range(2):
                            last = T.matmul(p[4][0:128, 0:nch],
                                            wp_sb[kk][:, 128 * m:128 * (m + 1)],
                                            ph_sb[kk][:, 0:nch],
                                            start=(kk == 0), stop=(kk == 1))
                        tv = tg(p[5][0:1, 0:1])
                        S.wait_ge(ct.h, tv)
                        last = S.activation(out=pb_sb[m][:, 0:nch],
                                            in_=p[4][0:128, 0:nch], func=AF.Identity,
                                            bias=bias2["bp_t"][:, m:m + 1],
                                            scale=sc16[:, 0:1])
                        pbv = cs.inc(last)
                        T.wait_ge(cs.h, pbv)
                    pb_ready = cs.v
                    for t0 in range(0, Tq, 512):
                        n = min(512, Tq - t0)
                        for kk in range(2):
                            last = T.matmul(p[4][0:1, 0:n],
                                            wh_sb[kk][:, 2:3],
                                            IN[kk][:, 64 + t0:64 + t0 + n],
                                            start=(kk == 0), stop=(kk == 1))
                        tv = tg(p[5][0:1, 0:1])
                        S.wait_ge(ct.h, tv)
                        gv = cs.inc(S.activation(out=gate_row[0:1, t0:t0 + n],
                                                 in_=p[4][0:1, 0:n], func=AF.Sigmoid,
                                                 bias=bh_sb[0:1, 2:3], scale=1.0))
                        T.wait_ge(cs.h, gv)
                    gate_ready = cs.v

                    groups = [blocks[i:i + 4] for i in range(0, len(blocks), 4)]
                    out_marks = []
                    sm_ev_par = {}
                    sc_ev_par = {}
                    wt_ev_par = {}
                    prev_d1v = None
                    mx_ready = {}
                    mx_free = {}

                    def mx_ap(stg, c, n):
                        if stg == 0:
                            return mx_g[c][:, 0:n]
                        return zbuf[:, 512 * c:512 * c + n]

                    def emit_conv(g):
                        grp_g = groups[g]
                        gb0_g = grp_g[0][0]
                        gw_g = sum(q for _, q in grp_g)
                        stg = g % 2
                        if stg in mx_free:
                            V.wait_ge(ct.h, mx_free[stg])
                        for c in range(2):
                            base_g = 64 + gb0_g - 4
                            V.tensor_scalar(out=mx_ap(stg, c, gw_g),
                                            in0=IN[c][:, base_g:base_g + gw_g],
                                            scalar1=cw_sb[:, 5 * c:5 * c + 1],
                                            scalar2=bias2["cb_t"][:, c:c + 1],
                                            op0=ALU.mult, op1=ALU.add)
                            for kk in range(1, 5):
                                last_g = V.scalar_tensor_tensor(
                                    out=mx_ap(stg, c, gw_g),
                                    in0=IN[c][:, base_g + kk:base_g + kk + gw_g],
                                    scalar=cw_sb[:, 5 * c + kk:5 * c + kk + 1],
                                    in1=mx_ap(stg, c, gw_g),
                                    op0=ALU.mult, op1=ALU.add)
                        mx_ready[g] = cv.inc(last_g)

                    emit_conv(0)
                    for gi_c, grp in enumerate(groups):
                        gb0 = grp[0][0]
                        gw = sum(q for _, q in grp)
                        cw_ = gw + 64
                        qc = slice(64 + gb0, 64 + gb0 + gw)
                        if 1 in sc_ev_par:
                            T.wait_ge(cv.h, sc_ev_par[1])
                        pos = 0
                        while pos < cw_:
                            n = min(512, cw_ - pos)
                            for m in range(2):
                                for kk in range(2):
                                    last = T.matmul(
                                        p[m][0:128, 0:n],
                                        wk_sb[kk][:, 128 * m:128 * (m + 1)],
                                        IN[kk][:, gb0 + pos:gb0 + pos + n],
                                        start=(kk == 0), stop=(kk == 1))
                                tv = tg(p[5][0:1, 0:1])
                                S.wait_ge(ct.h, tv)
                                ev = cs.inc(S.activation(
                                    out=k_g[m][:, pos:pos + n], in_=p[m][0:128, 0:n],
                                    func=AF.Identity,
                                    bias=bias2["bk_t"][:, m:m + 1], scale=1.0))
                                T.wait_ge(cs.h, ev)
                            pos += n
                        nv = cw_ // 64
                        if prev_d1v is not None:
                            T.wait_ge(cs.h, prev_d1v)
                        for t in range(nv):
                            bank = 2 + (t % 2)
                            for kk in range(2):
                                last = T.matmul(
                                    p[bank][0:64, 0:256],
                                    IN[kk][:, gb0 + 64 * t:gb0 + 64 * (t + 1)],
                                    wv_sb[kk][:], start=(kk == 0), stop=(kk == 1))
                            tv = tg(p[5][0:1, 0:1])
                            S.wait_ge(ct.h, tv)
                            ev = cs.inc(S.copy(out=v_g[t][:], in_=p[bank][0:64, 0:256]))
                            T.wait_ge(cs.h, ev)
                        kv_ready = cs.v
                        for m in range(2):
                            for kk in range(2):
                                last = T.matmul(
                                    p[m][0:128, 0:gw],
                                    wq_sb[kk][:, 128 * m:128 * (m + 1)],
                                    IN[kk][:, qc], start=(kk == 0), stop=(kk == 1))
                            tv = tg(p[5][0:1, 0:1])
                            S.wait_ge(ct.h, tv)
                            ev = cs.inc(S.activation(
                                out=q_g[m][:, 0:gw], in_=p[m][0:128, 0:gw],
                                func=AF.Identity, bias=bias2["bq_t"][:, m:m + 1],
                                scale=1.0))
                            T.wait_ge(cs.h, ev)
                        q_ready = cs.v
                        V.wait_ge(cs.h, q_ready)

                        scv_of = {}

                        def emit_scores(bi_):
                            ob, qlen = grp[bi_]
                            par = bi_ % 2
                            lc = ob - gb0
                            cl = qlen + 64
                            sb = p[4] if par == 0 else p[0]
                            if par in sc_ev_par:
                                T.wait_ge(cv.h, sc_ev_par[par])
                            for kk in range(2):
                                last = T.matmul(
                                    sb[0:qlen, 0:cl],
                                    q_g[kk][:, lc:lc + qlen],
                                    k_g[kk][:, lc:lc + cl],
                                    start=(kk == 0), stop=(kk == 1))
                            scv_of[bi_] = tg(p[5][0:1, 0:1])

                        emit_scores(0)
                        for bi_, (ob, qlen) in enumerate(grp):
                            par = bi_ % 2
                            lc = ob - gb0
                            cl = qlen + 64
                            mask = blk0m if ob == blk0_ob else bandm
                            sb = p[4] if par == 0 else p[0]
                            if bi_ + 1 < len(grp):
                                emit_scores(bi_ + 1)
                            V.wait_ge(ct.h, scv_of[bi_])
                            sc_ev_par[par] = cv.inc(V.scalar_tensor_tensor(
                                out=sc_sb[par][0:qlen, 0:cl],
                                in0=sb[0:qlen, 0:cl], scalar=1.0 / 16.0,
                                in1=mask[0:qlen, 0:cl], op0=ALU.mult, op1=ALU.add))
                            g8 = cv.inc(V.max(out=top8[par][0:qlen, :],
                                              in_=sc_sb[par][0:qlen, 0:cl]))
                            V.wait_ge(cv.h, g8)
                            nmv = cv.inc(V.tensor_scalar_mul(
                                negm[par][0:qlen, :], top8[par][0:qlen, 0:1], -1.0))
                            S.wait_ge(cv.h, nmv)
                            S.activation(out=ef_sb[par][0:qlen, 0:8],
                                         in_=top8[par][0:qlen, :], func=AF.Exp,
                                         bias=negm[par][0:qlen, :], scale=1.0,
                                         accum_out=den[par][0:qlen, :])
                            exv = cs.inc(S.activation(
                                out=ef_sb[par][0:qlen, 0:cl],
                                in_=sc_sb[par][0:qlen, 0:cl], func=AF.Exp,
                                bias=negm[par][0:qlen, :], scale=1.0))
                            V.wait_ge(cs.h, exv)
                            rdv = cv.inc(V.reciprocal(rd_[par][0:qlen, :],
                                                      den[par][0:qlen, :]))
                            V.wait_ge(cv.h, rdv)
                            V.tensor_single_scalar(mk_sb[par][0:qlen, 0:cl],
                                                   sc_sb[par][0:qlen, 0:cl],
                                                   top8[par][0:qlen, 7:8], ALU.is_ge)
                            wfv = cv.inc(V.scalar_tensor_tensor(
                                out=ef_sb[par][0:qlen, 0:cl],
                                in0=ef_sb[par][0:qlen, 0:cl],
                                scalar=rd_[par][0:qlen, :],
                                in1=mk_sb[par][0:qlen, 0:cl],
                                op0=ALU.mult, op1=ALU.mult))
                            T.wait_ge(cv.h, wfv)
                            if par in wt_ev_par:
                                T.wait_ge(cs.h, wt_ev_par[par])
                            npieces = cl // 64
                            for pi in range(npieces):
                                sl = pi * 128
                                last = T.transpose(
                                    p[6 + par][0:64, sl:sl + qlen],
                                    ef_sb[par][0:qlen, 64 * pi:64 * (pi + 1)],
                                    ident[0:qlen, 0:qlen])
                            trv = tg(p[5][0:1, 0:1])
                            S.wait_ge(ct.h, trv)
                            for pi in range(npieces):
                                sl = pi * 128
                                last = S.copy(out=wT64[par][pi][0:64, 0:qlen],
                                              in_=p[6 + par][0:64, sl:sl + qlen])
                            wev = cs.inc(last)
                            wt_ev_par[par] = wev
                            T.wait_ge(cs.h, wev)
                            for m in range(2):
                                off = m * 128
                                for pi in range(npieces):
                                    vt = lc // 64 + pi
                                    last = T.matmul(
                                        p[2 + par][0:128, off:off + qlen],
                                        v_g[vt][0:64, 128 * m:128 * (m + 1)],
                                        wT64[par][pi][0:64, 0:qlen],
                                        start=(pi == 0), stop=(pi == npieces - 1))
                            smv = tg(p[5][0:1, 0:1])
                            S.wait_ge(ct.h, smv)
                            for m in range(2):
                                off = m * 128
                                last = S.activation(
                                    out=sm_g[m][:, lc:lc + qlen],
                                    in_=p[2 + par][0:128, off:off + qlen],
                                    func=AF.Identity,
                                    bias=bias2["bv_t"][:, m:m + 1], scale=1.0)
                            smev = cs.inc(last)
                            sm_ev_par[par] = smev
                            if bi_ == 1 and gi_c + 1 < len(groups):
                                emit_conv(gi_c + 1)
                            if tok0_col is not None and ob == blk0_ob:
                                c0 = tok0_col - gb0
                                V.wait_ge(cs.h, smev)
                                V.tensor_single_scalar(sm_g[0][:, c0:c0 + 1],
                                                       sm_g[0][:, c0:c0 + 1],
                                                       t0scal[:], ALU.mult)
                                cv.inc(V.tensor_single_scalar(
                                    sm_g[1][:, c0:c0 + 1], sm_g[1][:, c0:c0 + 1],
                                    t0scal[:], ALU.mult))
                        summ_ready = (cs.v, cv.v)

                        T.wait_ge(cs.h, summ_ready[0])
                        T.wait_ge(cv.h, summ_ready[1])
                        T.wait_ge(cv.h, mx_ready[gi_c])
                        T.wait_ge(cs.h, pb_ready)
                        pbc0 = (64 + gb0) // 16
                        rhs_list = [IN[0][:, qc], IN[1][:, qc],
                                    mx_ap(gi_c % 2, 0, gw), mx_ap(gi_c % 2, 1, gw),
                                    pb_sb[0][:, pbc0:pbc0 + gw // 16]
                                    .unsqueeze(2).broadcast_to([128, gw // 16, 16]),
                                    pb_sb[1][:, pbc0:pbc0 + gw // 16]
                                    .unsqueeze(2).broadcast_to([128, gw // 16, 16]),
                                    sm_g[0][:, 0:gw], sm_g[1][:, 0:gw]]
                        for m in range(2):
                            for kk in range(NKD):
                                last = T.matmul(p[m][0:128, 0:gw],
                                                w1a_sb[kk][:, 128 * m:128 * (m + 1)],
                                                rhs_list[kk], start=(kk == 0),
                                                stop=(kk == NKD - 1))
                            uiv = tg(p[5][0:1, 0:1])
                            mx_free[gi_c % 2] = uiv
                            S.wait_ge(ct.h, uiv)
                            thv = cs.inc(S.activation(
                                out=th_g[m][:, 0:gw], in_=p[m][0:128, 0:gw],
                                func=AF.Tanh, bias=bias2["b1a_t"][:, m:m + 1],
                                scale=1.0))
                            T.wait_ge(cs.h, thv)
                        for m in range(2):
                            for kk in range(2):
                                last = T.matmul(p[2 + m][0:128, 0:gw],
                                                w2a_sb[kk][:, 128 * m:128 * (m + 1)],
                                                th_g[kk][:, 0:gw], start=(kk == 0),
                                                stop=(kk == 1))
                        dv = tg(p[5][0:1, 0:1])
                        T.wait_ge(cs.h, gate_ready)
                        gbv = ct.inc(T.matmul(p[6][0:128, 0:gw], ones_r[:],
                                              gate_row[0:1, gb0:gb0 + gw],
                                              start=True, stop=True))
                        S.wait_ge(ct.h, dv)
                        for m in range(2):
                            last = S.activation(out=d1_g[m][:, 0:gw],
                                                in_=p[2 + m][0:128, 0:gw],
                                                func=AF.Identity,
                                                bias=bias2["b2a_t"][:, m:m + 1],
                                                scale=1.0)
                        d1v = cs.inc(last)
                        prev_d1v = d1v
                        V.wait_ge(cs.h, d1v)
                        V.wait_ge(ct.h, gbv)
                        for m in range(2):
                            V.tensor_mul(d1_g[m][:, 0:gw], d1_g[m][:, 0:gw],
                                         p[6][0:128, 0:gw])
                            last = V.tensor_add(OUT[m][:, gb0:gb0 + gw],
                                                d1_g[m][:, 0:gw], IN[m][:, qc])
                        out_marks.append(cv.inc(last))
                    return out_marks[-1]

                if KDUMP:
                    SY.wait_ge(cs.h, h0_ready[0])
                    SY.wait_ge(cv.h, h0_ready[1])
                    for m in range(2):
                        sout.inc(SY.dma_start(out=d_h0[m, :, :], in_=hA[m][:]), 16)
                blocks1 = [(0, 64)] + [(64 + 128 * b, 128) for b in range(16)]
                s1v = emit_attn_step(hA, hB, TH, blocks1, 64, 64, h0_ready)
                V.wait_ge(cv.h, s1v)
                V.tensor_single_scalar(hB[0][:, 0:64], hB[0][:, 0:64], hscal[:], ALU.mult)
                g = cv.inc(V.tensor_single_scalar(hB[1][:, 0:64], hB[1][:, 0:64],
                                                  hscal[:], ALU.mult))
                h1_ready = (cs.v, cv.v)

                if KDUMP:
                    SY.wait_ge(cs.h, h1_ready[0])
                    SY.wait_ge(cv.h, h1_ready[1])
                    for m in range(2):
                        sout.inc(SY.dma_start(out=d_h1[m, :, :], in_=hB[m][:]), 16)
                blocks2 = [(128 * b, 128) for b in range(16)]
                s2v = emit_attn_step(hB, hA, 2112, blocks2, 0, 0, h1_ready)

                if KDUMP:
                    SY.wait_ge(cv.h, s2v)
                    for m in range(2):
                        sout.inc(SY.dma_start(out=d_h2[m, :, :],
                                              in_=hA[m][:, 0:TOWN]), 16)
                x2_ev = {}
                for j in range(1, NXT):
                    hc = slice(128 * (j - 1), 128 * j)
                    bk = 2 * ((j - 1) % 2)
                    T.wait_ge(cv.h, s2v)
                    if j >= 3:
                        T.wait_ge(cv.h, x2_ev[j - 2])
                    for half in range(2):
                        for kk in range(2):
                            last = T.matmul(p[bk + half][0:128, 0:512],
                                            hA[kk][:, hc],
                                            wo_sb[kk][:, 512 * half:512 * (half + 1)],
                                            start=(kk == 0), stop=False)
                        last = T.matmul(p[bk + half][0:128, 0:512], ones_r[0:1, 0:128],
                                        bo_sb[0:1, 512 * half:512 * (half + 1)],
                                        start=False, stop=True)
                    uv = tg(p[5][0:1, 0:1])
                    V.wait_ge(ct.h, uv)
                    V.wait_ge(cv.h, zadd_ready)
                    V.tensor_scalar(out=zbuf[:], in0=x[j][:],
                                    scalar1=a2t[:, j:j + 1], scalar2=b2ct[:, j:j + 1],
                                    op0=ALU.mult, op1=ALU.add)
                    for half in range(2):
                        hcd = slice(512 * half, 512 * (half + 1))
                        last = V.scalar_tensor_tensor(
                            out=x[j][:, hcd], in0=p[bk + half][0:128, 0:512],
                            scalar=1.0,
                            in1=zbuf[:, hcd], op0=ALU.mult, op1=ALU.add)
                    x2_ev[j] = cv.inc(last)
                    st3_cp["cp"] = emit_stats_tile(j - 1, j, nmu3)

            if KDUMP:
                for j in range(1, NXT):
                    SY.wait_ge(cv.h, x2_ev[j])
                    sout.inc(SY.dma_start(out=d_x2[128 * j:128 * (j + 1), :],
                                          in_=x[j][:]), 16)
            barrier([V, S, T, G, SY])
            stats3_ready = emit_stats_fin(NXT - 1, rsd3, st3_cp["cp"])
            emit_ffn(list(range(1, NXT)), nmu3, rsd3, stats3_ready,
                     wait_none, True)

        SY.wait_ge(sout.h, sout.v)

    nc.finalize()
    return nc


_PROG = None


def _get_prog():
    global _PROG
    if _PROG is None:
        _PROG = build_program()
    return _PROG


def kernel(**inputs):
    f32 = np.float32
    P = {k: np.ascontiguousarray(np.asarray(v, f32)) for k, v in inputs.items()}
    x_full = P["L0_attn_x"].reshape(2 * 8192, D)

    def t2(b):
        return np.ascontiguousarray(b.reshape(2, 128).T)

    cw = P["L0_attn_conv_w"]
    cw_t = np.ascontiguousarray(
        np.transpose(cw.reshape(5, 2, 128), (2, 1, 0)).reshape(128, 10))
    common = {
        "wi": P["L0_attn_Wi"], "wq": P["L0_attn_Wq"], "wk": P["L0_attn_Wk"],
        "wv": P["L0_attn_Wv"], "wp": P["L0_attn_Wp"], "w2a": P["L0_attn_W2"],
        "w1a": P["L0_attn_W1"], "wh": P["L0_attn_Wh"], "wo": P["L0_attn_Wo"],
        "w1f": np.ascontiguousarray(P["L0_ffn_W1"].T),
        "w2f": np.ascontiguousarray(P["L0_ffn_W2"].T),
        "bi_t": t2(P["L0_attn_bi"]), "bq_t": t2(P["L0_attn_bq"]),
        "bk_t": t2(P["L0_attn_bk"]), "bv_t": t2(P["L0_attn_bv"]),
        "bp_t": t2(P["L0_attn_bp"]), "b2a_t": t2(P["L0_attn_b2"]),
        "b1a_t": t2(P["L0_attn_b1"]), "cb_t": t2(P["L0_attn_conv_b"]),
        "b1f_t": np.ascontiguousarray(P["L0_ffn_b1"].reshape(32, 128).T),
        "cw_t": cw_t,
        "bh_r": P["L0_attn_bh"].reshape(1, 3),
        "bo_r": P["L0_attn_bo"].reshape(1, D),
        "b2f_r": P["L0_ffn_b2"].reshape(1, D),
        "ident": np.eye(128, dtype=f32),
    }
    r = np.arange(128)[:, None]
    c = np.arange(192)[None, :]
    band = np.where((c >= r) & (c < r + 64), 0.0, NEG).astype(f32)
    blk0 = np.where((c >= r) & (c < r + 64) & (c >= 64), 0.0, NEG).astype(f32)

    in_maps = []
    for core in range(NCORE):
        start = TOWN * core
        first = core % 4 == 0
        xin = np.zeros((TH, D), f32)
        xin[HALO:] = x_full[start:start + TOWN]
        if not first:
            xin[0:HALO] = x_full[start - HALO:start]
        m = dict(common)
        m["xin"] = xin
        m["bandm"] = band
        m["blk0m"] = blk0 if first else band
        m["halo_scal"] = np.full((128, 1), 0.0 if first else 1.0, f32)
        m["tok0_scal"] = np.full((128, 1), 0.0 if first else 1.0, f32)
        in_maps.append(m)

    nc = _get_prog()
    res = run_bass_kernel_spmd(nc, in_maps, core_ids=list(range(NCORE)))
    out = np.concatenate([np.asarray(res.results[cc]["xout"])
                          for cc in range(NCORE)], axis=0)
    return out.reshape(2, 8192, D)


# version 12
# speedup vs baseline: 1.0910x; 1.0455x over previous
import math
from contextlib import ExitStack

import numpy as np

import concourse.bass as bass
import concourse.mybir as mybir
from concourse.bass_utils import run_bass_kernel_spmd

F32 = mybir.dt.float32
BF16 = mybir.dt.bfloat16
AF = mybir.ActivationFunctionType
ALU = mybir.AluOpType
AX = mybir.AxisListType

NEG = -1.0e30
NCORE = 8
TOWN = 2048
HALO = 128
TH = TOWN + HALO
NXT = TH // 128
D, DS, FF, LB = 1024, 256, 4096, 64
NKD = D // 128
NKF = FF // 128


class Ctr:
    __slots__ = ("h", "v")

    def __init__(self, nc, name):
        self.h = nc.alloc_semaphore(name)
        self.v = 0

    def inc(self, instr, n=1):
        instr.then_inc(self.h, n)
        self.v += n
        return self.v


def build_program():
    nc = bass.Bass("TRN2", target_bir_lowering=False,
                   detect_race_conditions=False)

    def din(name, shape, dt=F32):
        return nc.dram_tensor(name, shape, dt, kind="ExternalInput").ap()

    xin = din("xin", [TH, D])
    wi_d = din("wi", [D, DS])
    wq_d = din("wq", [DS, DS])
    wk_d = din("wk", [DS, DS])
    wv_d = din("wv", [DS, DS])
    wp_d = din("wp", [DS, DS])
    w2a_d = din("w2a", [DS, DS])
    w1a_d = din("w1a", [D, DS])
    wh_d = din("wh", [DS, 3])
    wo_d = din("wo", [DS, D])
    w1f_d = din("w1f", [D, FF])
    w2f_d = din("w2f", [FF, D])
    bias2_d = {}
    for nm in ("bi_t", "bq_t", "bk_t", "bv_t", "bp_t", "b2a_t", "b1a_t", "cb_t"):
        bias2_d[nm] = din(nm, [128, 2])
    b1f_d = din("b1f_t", [128, 32])
    cw_d = din("cw_t", [128, 10])
    bh_d = din("bh_r", [1, 3])
    bo_d = din("bo_r", [1, D])
    b2f_d = din("b2f_r", [1, D])
    ident_d = din("ident", [128, 128])
    bandm_d = din("bandm", [128, 192])
    blk0m_d = din("blk0m", [128, 192])
    hscal_d = din("halo_scal", [128, 1])
    t0scal_d = din("tok0_scal", [128, 1])

    import os as _os
    KDUMP = bool(_os.environ.get("KDUMP"))
    xout = nc.dram_tensor("xout", [TOWN, D], F32, kind="ExternalOutput").ap()
    if _os.environ.get("KFFN1"):
        d_st = nc.dram_tensor("d_st", [128, 32], F32, kind="ExternalOutput").ap()
        d_w1tl8 = nc.dram_tensor("d_w1tl8", [NKD, 128, FF], BF16, kind="ExternalOutput").ap()
        d_f1all = None
    if _os.environ.get("KF1"):
        d_f1all = nc.dram_tensor("d_f1all", [9 * NKF, 128, 256], F32, kind="ExternalOutput").ap()
    if KDUMP:
        d_x1 = nc.dram_tensor("d_x1", [TH, D], F32, kind="ExternalOutput").ap()
        d_h0 = nc.dram_tensor("d_h0", [2, 128, TH], F32, kind="ExternalOutput").ap()
        d_h1 = nc.dram_tensor("d_h1", [2, 128, 2112], F32, kind="ExternalOutput").ap()
        d_h2 = nc.dram_tensor("d_h2", [2, 128, TOWN], F32, kind="ExternalOutput").ap()
        d_x2 = nc.dram_tensor("d_x2", [TH, D], F32, kind="ExternalOutput").ap()

    scr_kind = {"kind": "ExternalOutput"} if KDUMP else {}
    w1scr = nc.dram_tensor("w1scr", [D, FF], BF16, **scr_kind).ap()
    w2scr = nc.dram_tensor("w2scr", [FF, D], BF16, **scr_kind).ap()
    if KDUMP:
        d_tern = nc.dram_tensor("d_tern", [128, 8], F32, kind="ExternalOutput").ap()
        d_w1tl = nc.dram_tensor("d_w1tl", [128, FF], BF16, kind="ExternalOutput").ap()
        d_w2T = nc.dram_tensor("d_w2T", [128, D], BF16, kind="ExternalOutput").ap()
        d_yhi = nc.dram_tensor("d_yhi", [128, 256], BF16, kind="ExternalOutput").ap()
        d_ylo = nc.dram_tensor("d_ylo", [128, 256], BF16, kind="ExternalOutput").ap()
        d_f1 = nc.dram_tensor("d_f1", [128, 256], F32, kind="ExternalOutput").ap()
        d_bst = nc.dram_tensor("d_bst", [128, 12], F32, kind="ExternalOutput").ap()
        d_stA = nc.dram_tensor("d_stA", [128, 34], F32, kind="ExternalOutput").ap()
        d_stC = nc.dram_tensor("d_stC", [128, 34], F32, kind="ExternalOutput").ap()

    def sal(name, shape, dt=F32):
        return nc.alloc_sbuf_tensor(name, shape, dt)

    x = [sal(f"x{j}", [128, D]) for j in range(NXT)]

    wq_sb = [sal(f"wq{k}", [128, DS]) for k in range(2)]
    wk_sb = [sal(f"wk{k}", [128, DS]) for k in range(2)]
    wv_sb = [sal(f"wv{k}", [128, DS]) for k in range(2)]
    wp_sb = [sal(f"wp{k}", [128, DS]) for k in range(2)]
    w2a_sb = [sal(f"w2a{k}", [128, DS]) for k in range(2)]

    wh_sb = [sal(f"wh{k}", [128, 3]) for k in range(2)]
    wo_sb = [sal(f"wo{k}", [128, D]) for k in range(2)]
    bias2 = {nm: sal(nm + "_s", [128, 2]) for nm in
             ("bi_t", "bq_t", "bk_t", "bv_t", "bp_t", "b2a_t", "b1a_t", "cb_t")}
    b1f_sb = sal("b1f_s", [128, 32])
    cw_sb = sal("cw_s", [128, 10])
    bh_sb = sal("bh_s", [1, 3])
    bo_sb = sal("bo_s", [1, D])
    b2f_sb = sal("b2f_s", [1, D])
    ident = sal("ident_s", [128, 128])
    bandm = sal("bandm_s", [128, 192])
    blk0m = sal("blk0m_s", [128, 192])
    hscal = sal("hscal_s", [128, 1])
    t0scal = sal("t0scal_s", [128, 1])
    ones_r = sal("ones_r", [1, 128])
    zrow16 = sal("zrow16", [1, 8], BF16)
    ones_c = sal("ones_c", [128, 1])
    eps_t = sal("eps_t", [128, 1])
    sc16 = sal("sc16", [128, 1])

    nmu1 = sal("nmu1", [128, NXT]); rsd1 = sal("rsd1", [128, NXT])
    nmu2 = sal("nmu2", [128, NXT]); rsd2 = sal("rsd2", [128, NXT])
    nmu3 = sal("nmu3", [128, 16]);  rsd3 = sal("rsd3", [128, 16])
    a2t = sal("a2t", [128, NXT]); b2ct = sal("b2ct", [128, NXT])
    var_all = sal("var_all", [128, NXT])
    sd_all = sal("sd_all", [128, NXT])
    bst = sal("bst", [128, 12])
    if KDUMP:
        bst12 = sal("bst12", [128, 12])
        d_stA_sb = sal("d_stA_sb", [128, 34])
    bagg = sal("bagg", [128, 2])
    rtmp = sal("rtmp", [128, 32])
    racc = sal("racc", [128, 1])
    tot1 = sal("tot1", [1, 1])
    half1 = sal("half1", [128, 1]); nhalf1 = sal("nhalf1", [128, 1])
    half2 = sal("half2", [128, 1]); nhalf2 = sal("nhalf2", [128, 1])

    p = [nc.alloc_psum_tensor(f"p{i}", [128, 512], F32) for i in range(8)]

    sx = Ctr(nc, "sx")
    swa = Ctr(nc, "swa")
    swr = Ctr(nc, "swr")
    sscr = Ctr(nc, "sscr")
    swt = Ctr(nc, "swt")
    sout = Ctr(nc, "sout")
    cv = Ctr(nc, "cv")
    cs = Ctr(nc, "cs")
    ct = Ctr(nc, "ct")
    cg = Ctr(nc, "cg")

    with nc.Block() as _blk:
        V, S, T, G, SY = nc.vector, nc.scalar, nc.tensor, nc.gpsimd, nc.sync


        def tg(ap, n=1):
            return ct.inc(T.matmul(ap, zrow16[0:1, 0:1], zrow16[0:1, 0:1],
                                   start=False, stop=True,
                                   skip_group_check=True), n)

        def barrier(engines):
            for e in engines:
                for c in (cv, cs, ct, cg):
                    e.wait_ge(c.h, c.v)

        for k in range(2):
            r = slice(128 * k, 128 * (k + 1))
            swa.inc(G.dma_start(out=wq_sb[k][:], in_=wq_d[r, :]), 16)
            swa.inc(G.dma_start(out=wk_sb[k][:], in_=wk_d[r, :]), 16)
            swa.inc(G.dma_start(out=wv_sb[k][:], in_=wv_d[r, :]), 16)
            swa.inc(G.dma_start(out=wp_sb[k][:], in_=wp_d[r, :]), 16)
            swa.inc(G.dma_start(out=w2a_sb[k][:], in_=w2a_d[r, :]), 16)
            swa.inc(G.dma_start(out=wh_sb[k][:], in_=wh_d[r, :]), 16)
            swa.inc(G.dma_start(out=wo_sb[k][:], in_=wo_d[r, :]), 16)
        for nm, dd in bias2_d.items():
            swa.inc(G.dma_start(out=bias2[nm][:], in_=dd), 16)
        for dst, src in ((b1f_sb, b1f_d), (cw_sb, cw_d), (bh_sb, bh_d),
                         (bo_sb, bo_d), (b2f_sb, b2f_d), (ident, ident_d),
                         (bandm, bandm_d), (blk0m, blk0m_d), (hscal, hscal_d),
                         (t0scal, t0scal_d)):
            swa.inc(G.dma_start(out=dst[:], in_=src), 16)
        swa_done = swa.v

        V.memset(ones_r[:], 1.0)
        V.memset(zrow16[:], 0.0)
        V.memset(ones_c[:], 1.0)
        V.memset(eps_t[:], 1e-5)
        cv.inc(V.memset(sc16[:], 1.0 / 16.0))
        consts_ready = cv.v
        T.wait_ge(cv.h, consts_ready)
        T.matmul(p[5][0:1, 0:1], zrow16[0:1, 0:1], zrow16[0:1, 0:1],
                 start=True, stop=True, skip_group_check=True)

        def emit_stats_tile(jj, j, nmu_t):
            g0 = cv.inc(V.bn_stats(bst[:, 0:6], x[j][:, 0:512]))
            V.wait_ge(cv.h, g0)
            g1 = cv.inc(V.bn_stats(bst[:, 6:12], x[j][:, 512:1024]))
            V.wait_ge(cv.h, g1)
            g2 = cv.inc(V.bn_aggr(bagg[:], bst[:]))
            V.wait_ge(cv.h, g2)
            V.tensor_scalar_mul(nmu_t[:, jj:jj + 1], bagg[:, 0:1], -1.0)
            return V.tensor_copy(out=var_all[:, jj:jj + 1], in_=bagg[:, 1:2])

        def emit_stats_fin(n, rsd_t, last_cp):
            g3 = cv.inc(last_cp)
            S.wait_ge(cv.h, g3)
            s1 = cs.inc(S.activation(out=sd_all[:, 0:n], in_=var_all[:, 0:n],
                                     func=AF.Sqrt, bias=eps_t[:, 0:1], scale=1.0))
            V.wait_ge(cs.h, s1)
            g4 = cv.inc(V.reciprocal(rsd_t[:, 0:n], sd_all[:, 0:n]))
            V.wait_ge(cv.h, g4)
            return cv.v

        def emit_stats(tiles, nmu_t, rsd_t, wait_fn):
            for jj, j in enumerate(tiles):
                wait_fn(jj, j)
                last_cp = emit_stats_tile(jj, j, nmu_t)
            return emit_stats_fin(len(tiles), rsd_t, last_cp)

        def wait_x(jj, j):
            V.wait_ge(sx.h, 32 * NXT)

        with ExitStack() as es:
            w1raw = [es.enter_context(nc.sbuf_tensor(f"w1raw{i}", [128, 1024], F32))
                     for i in range(7)]
            w2raw = [es.enter_context(nc.sbuf_tensor(f"w2raw{i}", [128, 1024], F32))
                     for i in range(7)]
            tmpa = [es.enter_context(nc.sbuf_tensor(f"tmpa{i}", [128, 1024], F32))
                    for i in range(2)]
            tmpb1 = [es.enter_context(nc.sbuf_tensor(f"tmpb1_{i}", [128, 1024], F32))
                     for i in range(2)]
            tmpb2 = [es.enter_context(nc.sbuf_tensor(f"tmpb2_{i}", [128, 1024], F32))
                     for i in range(2)]
            t16a = [es.enter_context(nc.sbuf_tensor(f"t16a{i}", [128, 1024], BF16))
                    for i in range(3)]
            t16b = [es.enter_context(nc.sbuf_tensor(f"t16b{i}", [128, 1024], BF16))
                    for i in range(3)]
            rtmp2 = es.enter_context(nc.sbuf_tensor("rtmp2", [128, 32], F32))
            racc2 = es.enter_context(nc.sbuf_tensor("racc2", [128, 1], F32))
            tot2 = es.enter_context(nc.sbuf_tensor("tot2", [1, 1], F32))

            w1slices = [(r, s) for r in range(NKD) for s in range(4)]
            nW = len(w1slices)
            PF = 6
            w1A_dv = []
            w2A_dv = []
            w1A_red_done = []
            w2A_red_done = []

            def issue_w1(i, dvl):
                r, s = w1slices[i]
                for q4 in range(2):
                    cq = slice(512 * q4, 512 * (q4 + 1))
                    last = swr.inc(SY.dma_start(
                        out=w1raw[i % 7][:, cq],
                        in_=w1f_d[128 * r:128 * (r + 1),
                                  1024 * s + 512 * q4:1024 * s + 512 * (q4 + 1)]),
                        16)
                dvl.append(last)

            def issue_w2(i, dvl):
                for q4 in range(2):
                    cq = slice(512 * q4, 512 * (q4 + 1))
                    last = swr.inc(SY.dma_start(
                        out=w2raw[i % 7][:, cq],
                        in_=w2f_d[128 * i:128 * (i + 1), cq]), 16)
                dvl.append(last)

            for i in range(PF):
                issue_w1(i, w1A_dv)
                issue_w2(i, w2A_dv)
            for j in range(nW):
                if j < NXT:
                    r_ = slice(128 * j, 128 * (j + 1))
                    sx.inc(SY.dma_start(out=x[j][:, 0:512],
                                        in_=xin[r_, 0:512]), 16)
                    sx.inc(SY.dma_start(out=x[j][:, 512:1024],
                                        in_=xin[r_, 512:1024]), 16)
                i = j + PF
                if i < nW:
                    if i >= 7:
                        SY.wait_ge(cs.h, w1A_red_done[i - 7])
                        SY.wait_ge(cv.h, w2A_red_done[i - 7])
                    issue_w1(i, w1A_dv)
                    issue_w2(i, w2A_dv)
                S.wait_ge(swr.h, w1A_dv[min(j + 3, nW - 1)])
                if j == nW - 1:
                    S.wait_ge(sx.h, 32)
                    d0 = cs.inc(S.activation(out=tmpa[0][:], in_=x[0][:],
                                             func=AF.Abs, bias=0.0, scale=1.0))
                    S.wait_ge(cs.h, d0)
                w1A_red_done.append(cs.inc(S.activation(
                    out=tmpa[j % 2][:], in_=w1raw[j % 7][:], func=AF.Abs,
                    bias=0.0, scale=1.0, accum_out=rtmp[:, j:j + 1])))
                S.wait_ge(cs.h, w1A_red_done[-1])
                V.wait_ge(swr.h, w2A_dv[min(j + 3, nW - 1)])
                if j == nW - 1:
                    V.wait_ge(sx.h, 32)
                    d0 = cv.inc(V.tensor_copy(out=tmpb1[0][:], in_=x[0][:]))
                    V.wait_ge(cv.h, d0)
                w2A_red_done.append(cv.inc(V.reduce_sum(
                    rtmp2[:, j:j + 1], w2raw[j % 7][:], axis=AX.X,
                    apply_absolute_value=True)))
                V.wait_ge(cv.h, w2A_red_done[-1])
            V.wait_ge(cs.h, w1A_red_done[-1])
            g = cv.inc(V.reduce_sum(racc[:], rtmp[:], axis=AX.X))
            V.wait_ge(cv.h, g)
            w1A_red = cv.v
            g = cv.inc(V.reduce_sum(racc2[:], rtmp2[:], axis=AX.X))
            V.wait_ge(cv.h, g)
            w2A_red = cv.v

            T.wait_ge(cv.h, w1A_red)
            T.wait_ge(cv.h, w2A_red)
            T.matmul(p[0][0:1, 0:1], racc[:], ones_c[:], start=True, stop=True)
            T.matmul(p[1][0:1, 0:1], racc2[:], ones_c[:], start=True, stop=True)
            t1 = tg(p[5][0:1, 0:1])
            V.wait_ge(ct.h, t1)
            V.tensor_scalar(out=tot1[:], in0=p[0][0:1, 0:1],
                            scalar1=1.0 / (FF * D), scalar2=1e-8,
                            op0=ALU.mult, op1=ALU.max)
            g = cv.inc(V.tensor_scalar(out=tot2[:], in0=p[1][0:1, 0:1],
                                       scalar1=1.0 / (FF * D), scalar2=1e-8,
                                       op0=ALU.mult, op1=ALU.max))
            T.wait_ge(cv.h, g)
            T.matmul(p[0][0:128, 0:1], ones_r[:], tot1[:], start=True, stop=True)
            T.matmul(p[1][0:128, 0:1], ones_r[:], tot2[:], start=True, stop=True)
            t2 = tg(p[5][0:1, 0:1])
            V.wait_ge(ct.h, t2)
            V.tensor_scalar_mul(half1[:], p[0][0:128, 0:1], 0.5)
            V.tensor_scalar_mul(nhalf1[:], p[0][0:128, 0:1], -0.5)
            V.tensor_scalar_mul(half2[:], p[1][0:128, 0:1], 0.5)
            g = cv.inc(V.tensor_scalar_mul(nhalf2[:], p[1][0:128, 0:1], -0.5))
            V.wait_ge(cv.h, g)
            G.wait_ge(cv.h, g)
            scal_ready_v = cv.v

            w1t_done = []
            w1scr_dma = []
            w1B_dv = []
            w2t_done = []
            w2scr_dma = []
            w2B_dv = []
            SY.wait_ge(cs.h, w1A_red_done[-1])
            SY.wait_ge(cv.h, w2A_red_done[-1])
            for i in range(PF):
                issue_w1(i, w1B_dv)
                issue_w2(i, w2B_dv)
            for j in range(nW):
                i = j + PF
                if i < nW:
                    if i >= 7:
                        SY.wait_ge(cv.h, w1t_done[i - 7])
                        SY.wait_ge(cv.h, w2t_done[i - 7])
                    issue_w1(i, w1B_dv)
                    issue_w2(i, w2B_dv)
                rj, sj = w1slices[j]
                V.wait_ge(swr.h, w1B_dv[min(j + 3, nW - 1)])
                if j == nW - 1:
                    V.wait_ge(sx.h, 32)
                    d0 = cv.inc(V.tensor_copy(out=tmpb1[0][:], in_=x[0][:]))
                    V.wait_ge(cv.h, d0)
                if j >= 3:
                    V.wait_ge(sscr.h, w1scr_dma[j - 3])
                V.tensor_single_scalar(tmpb1[j % 2][:], w1raw[j % 7][:],
                                       nhalf1[:], ALU.is_le)
                w1t_done.append(cv.inc(V.scalar_tensor_tensor(
                    out=t16a[j % 3][:], in0=w1raw[j % 7][:], scalar=half1[:],
                    in1=tmpb1[j % 2][:], op0=ALU.is_ge, op1=ALU.subtract)))
                G.wait_ge(cv.h, w1t_done[j])
                w1scr_dma.append(sscr.inc(G.dma_start(
                    out=w1scr[128 * rj:128 * (rj + 1), 1024 * sj:1024 * (sj + 1)],
                    in_=t16a[j % 3][:]), 16))
                V.wait_ge(swr.h, w2B_dv[min(j + 3, nW - 1)])
                if j == nW - 1:
                    V.wait_ge(sx.h, 32)
                    d0 = cv.inc(V.tensor_copy(out=tmpb2[0][:], in_=x[0][:]))
                    V.wait_ge(cv.h, d0)
                if j >= 3:
                    V.wait_ge(sscr.h, w2scr_dma[j - 3])
                V.tensor_single_scalar(tmpb2[j % 2][:], w2raw[j % 7][:],
                                       nhalf2[:], ALU.is_le)
                w2t_done.append(cv.inc(V.scalar_tensor_tensor(
                    out=t16b[j % 3][:], in0=w2raw[j % 7][:], scalar=half2[:],
                    in1=tmpb2[j % 2][:], op0=ALU.is_ge, op1=ALU.subtract)))
                G.wait_ge(cv.h, w2t_done[j])
                w2scr_dma.append(sscr.inc(G.dma_start(
                    out=w2scr[128 * j:128 * (j + 1), :],
                    in_=t16b[j % 3][:]), 16))
            w1scr_all = w1scr_dma[-1]
            w2scr_all = w2scr_dma[-1]
            tern_done = (cv.v, cg.v, sscr.v)

        stats1_ready = emit_stats(list(range(NXT)), nmu1, rsd1, wait_x)


        if KDUMP:
            V.tensor_copy(out=d_stA_sb[:, 0:17], in_=nmu1[:])
            gC = cv.inc(V.tensor_copy(out=d_stA_sb[:, 17:34], in_=rsd1[:]))
            SY.wait_ge(cv.h, gC)
            sout.inc(SY.dma_start(out=d_stC, in_=d_stA_sb[:]), 16)
        if KDUMP:
            V.wait_ge(cv.h, scal_ready_v)
            V.tensor_copy(out=rtmp[:, 0:1], in_=half1[:])
            V.tensor_copy(out=rtmp[:, 1:2], in_=nhalf1[:])
            V.tensor_copy(out=rtmp[:, 2:3], in_=half2[:])
            V.tensor_copy(out=rtmp[:, 3:4], in_=nhalf2[:])
            V.wait_ge(cv.h, stats1_ready)
            V.tensor_copy(out=rtmp[:, 4:5], in_=nmu1[:, 0:1])
            V.tensor_copy(out=rtmp[:, 5:6], in_=rsd1[:, 0:1])
            V.tensor_copy(out=rtmp[:, 6:7], in_=nmu1[:, 16:17])
            gdv = cv.inc(V.tensor_copy(out=rtmp[:, 7:8], in_=rsd1[:, 16:17]))
            SY.wait_ge(cv.h, gdv)
            sout.inc(SY.dma_start(out=d_tern, in_=rtmp[:, 0:8]), 16)
        import os as _os
        if _os.environ.get("KDEBUG"):
            print("sbuf before ffn:", nc.sbuf_bytes_remaining)
        KSER = _os.environ.get("KSER", "")
        w1tl_stash = []
        def emit_ffn(tiles, nmu_t, rsd_t, stats_ready, xtile_wait, is_ffn2,
                     stats_cb=None):
            groups = []
            i = 0
            while i < len(tiles):
                groups.append(tiles[i:i + 2])
                i += 2
            with ExitStack() as es:
                w1tl = [es.enter_context(nc.sbuf_tensor(f"w1tl{i}_{is_ffn2}", [128, FF], BF16))
                        for i in range(NKD)]
                w1tl_stash.clear(); w1tl_stash.extend(w1tl)
                w2T = [es.enter_context(nc.sbuf_tensor(f"w2T{i}_{is_ffn2}", [128, D], BF16))
                       for i in range(5)]
                ybuf = [es.enter_context(nc.sbuf_tensor(f"ybuf{i}_{is_ffn2}", [128, D], F32))
                        for i in range(1)]
                yhi = [[es.enter_context(nc.sbuf_tensor(f"yhi{s}_{i}_{is_ffn2}", [128, 256], BF16))
                        for i in range(NKD)] for s in range(2)]
                ylo = [[es.enter_context(nc.sbuf_tensor(f"ylo{s}_{i}_{is_ffn2}", [128, 256], BF16))
                        for i in range(NKD)] for s in range(2)]
                f1f = [es.enter_context(nc.sbuf_tensor(f"f1f{i}_{is_ffn2}", [128, 256], F32))
                       for i in range(2)]
                f1h = [es.enter_context(nc.sbuf_tensor(f"f1h{i}_{is_ffn2}", [128, 256], BF16))
                       for i in range(2)]
                f1l = [es.enter_context(nc.sbuf_tensor(f"f1l{i}_{is_ffn2}", [128, 256], BF16))
                       for i in range(2)]

                SY.wait_ge(sscr.h, w2scr_all)
                V.wait_ge(sscr.h, w2scr_all)
                S.wait_ge(sscr.h, w2scr_all)
                w1tl_dma = []
                for k in range(NKD):
                    for q_ in range(4):
                        cq = slice(1024 * q_, 1024 * (q_ + 1))
                        w1tl_dma.append(swt.inc(SY.dma_start(
                            out=w1tl[k][:, cq],
                            in_=w1scr[128 * k:128 * (k + 1), cq]), 16))

                stats_pending = []
                w2T_load = {}
                w2T_free = {}
                mm1_free = {}
                split_free = {}
                ybuf_free = {}
                tr_free = {}
                out2_free = {}
                ev_done = {}
                fidx = 0

                split_ready = {}

                def emit_prep_tile(g, lt):
                    txs_g = groups[g]
                    stg = g % 2
                    j = txs_g[lt]
                    jj = tiles.index(j)
                    V.wait_ge(cv.h, stats_ready)
                    if stg in split_free:
                        S.wait_ge(ct.h, split_free[stg])
                        V.wait_ge(ct.h, split_free[stg])
                    xtile_wait(V, j)
                    if ybuf_free:
                        V.wait_ge(ct.h, max(ybuf_free.values()))
                    yv = cv.inc(V.tensor_scalar(
                        out=ybuf[0][:], in0=x[j][:],
                        scalar1=nmu_t[:, jj:jj + 1],
                        scalar2=rsd_t[:, jj:jj + 1],
                        op0=ALU.add, op1=ALU.mult))
                    for w in (0, 1):
                        bank = 6 + w
                        T.wait_ge(cv.h, yv)
                        if bank in tr_free:
                            T.wait_ge(cv.h, tr_free[bank])
                        for c in range(4 * w, 4 * w + 4):
                            sl = (c % 4) * 128
                            last = T.transpose(
                                p[bank][0:128, sl:sl + 128],
                                ybuf[0][:, 128 * c:128 * (c + 1)], ident[:])
                        tv = ct.inc(last)
                        ybuf_free[(g, lt)] = tv
                        S.wait_ge(ct.h, tv)
                        for c in range(4 * w, 4 * w + 4):
                            sl = (c % 4) * 128
                            last = S.activation(
                                out=yhi[stg][c][:, 128 * lt:128 * lt + 128],
                                in_=p[bank][0:128, sl:sl + 128],
                                func=AF.Identity, bias=0.0, scale=1.0)
                        sv = cs.inc(last)
                        V.wait_ge(cs.h, sv)
                        for c in range(4 * w, 4 * w + 4):
                            sl = (c % 4) * 128
                            last = V.tensor_sub(
                                ylo[stg][c][:, 128 * lt:128 * lt + 128],
                                p[bank][0:128, sl:sl + 128],
                                yhi[stg][c][:, 128 * lt:128 * lt + 128])
                        tr_free[bank] = cv.inc(last)
                    if lt == len(txs_g) - 1:
                        split_ready[stg] = (cs.v, cv.v)

                for lt_ in range(len(groups[0])):
                    emit_prep_tile(0, lt_)
                if KDUMP and not is_ffn2:
                    SY.wait_ge(cs.h, split_ready[0][0])
                    SY.wait_ge(cv.h, split_ready[0][1])
                    SY.wait_ge(swt.h, w1tl_dma[-1])
                    sout.inc(SY.dma_start(out=d_w1tl, in_=w1tl[0][:]), 16)
                    sout.inc(SY.dma_start(out=d_yhi, in_=yhi[0][0][:]), 16)
                    sout.inc(SY.dma_start(out=d_ylo, in_=ylo[0][0][:]), 16)

                for gi, txs in enumerate(groups):
                    NT = 128 * len(txs)
                    st = gi % 2
                    lov_of = {}

                    ngq = 32 * len(groups)

                    def emit_mm2(fq, gq):
                        T.wait_ge(cv.h, lov_of[fq])
                        T.wait_ge(swt.h, w2T_load[min(gq + 2, ngq - 1)])
                        if fq == 0:
                            for b in range(2 * len(txs)):
                                if b in out2_free:
                                    T.wait_ge(cv.h, out2_free[b])
                        fbq = fq % 2
                        for lt in range(len(txs)):
                            for sidx, srcq in enumerate((f1h, f1l)):
                                for half in range(2):
                                    lastq = T.matmul(
                                        p[2 * lt + half][0:128, 0:512],
                                        srcq[fbq][:, 128 * lt:128 * lt + 128],
                                        w2T[gq % 5][:, 512 * half:512 * (half + 1)],
                                        start=(fq == 0 and sidx == 0),
                                        stop=False)
                        w2T_free[gq] = ct.inc(lastq)
                        nq = gq + 5
                        if nq < ngq:
                            fq3 = nq % NKF
                            SY.wait_ge(ct.h, w2T_free[gq])
                            w2T_load[nq] = swt.inc(SY.dma_start(
                                out=w2T[nq % 5][:],
                                in_=w2scr[128 * fq3:128 * (fq3 + 1), :]), 16)

                    for f in range(NKF):
                        if fidx == 0:
                            for q_ in range(5):
                                w2T_load[q_] = swt.inc(SY.dma_start(
                                    out=w2T[q_ % 5][:],
                                    in_=w2scr[128 * q_:128 * (q_ + 1), :]), 16)
                        bank = 4 + (f % 2)
                        if f == 0:
                            T.wait_ge(swt.h, w1tl_dma[-1])
                            T.wait_ge(cs.h, split_ready[st][0])
                            T.wait_ge(cv.h, split_ready[st][1])
                        if bank in mm1_free:
                            T.wait_ge(cs.h, mm1_free[bank])
                        nmm = 0
                        for kk in range(NKD):
                            for src in (yhi, ylo):
                                last = T.matmul(
                                    p[bank][0:128, 0:NT],
                                    w1tl[kk][:, 128 * f:128 * (f + 1)],
                                    src[st][kk][:, 0:NT],
                                    start=(nmm == 0), stop=(nmm == 2 * NKD - 1))
                                nmm += 1
                        mm1v = ct.inc(last)
                        if f == NKF - 1:
                            split_free[st] = mm1v
                        fb = f % 2
                        S.wait_ge(ct.h, mm1v)
                        S.activation(out=f1f[fb][:, 0:NT], in_=p[bank][0:128, 0:NT],
                                     func=AF.Gelu, bias=b1f_sb[:, f:f + 1], scale=1.0)
                        mm1_free[bank] = cs.inc(S.activation(
                            out=f1h[fb][:, 0:NT], in_=f1f[fb][:, 0:NT],
                            func=AF.Identity, bias=0.0, scale=1.0))
                        V.wait_ge(cs.h, mm1_free[bank])
                        lov_of[f] = cv.inc(V.tensor_sub(f1l[fb][:, 0:NT],
                                                        f1f[fb][:, 0:NT],
                                                        f1h[fb][:, 0:NT]))
                        if f == 8 and gi + 1 < len(groups):
                            emit_prep_tile(gi + 1, 0)
                        if f == 20 and gi + 1 < len(groups) \
                                and len(groups[gi + 1]) > 1:
                            emit_prep_tile(gi + 1, 1)
                        if f in (14, 26) and stats_pending:
                            stats_cb(stats_pending.pop(0))
                        if f >= 1:
                            emit_mm2(f - 1, fidx - 1)
                        fidx += 1
                    emit_mm2(NKF - 1, fidx - 1)
                    for lt in range(len(txs)):
                        for half in range(2):
                            last = T.matmul(
                                p[2 * lt + half][0:128, 0:512],
                                ones_r[0:1, 0:128],
                                b2f_sb[0:1, 512 * half:512 * (half + 1)],
                                start=False, stop=True)
                    bv_ = ct.inc(last)
                    if "c" in KSER:
                        barrier([V, S, T, G, SY])
                    V.wait_ge(ct.h, bv_)
                    for lt, j in enumerate(txs):
                        for half in range(2):
                            hc = slice(512 * half, 512 * (half + 1))
                            last = V.scalar_tensor_tensor(
                                out=x[j][:, hc], in0=p[2 * lt + half][0:128, 0:512],
                                scalar=0.5, in1=x[j][:, hc],
                                op0=ALU.mult, op1=ALU.add)
                            out2_free[2 * lt + half] = cv.inc(last)
                        ev_done[j] = cv.v
                        if stats_cb is not None:
                            stats_pending.append(j)
                        if is_ffn2:
                            SY.wait_ge(cv.h, ev_done[j])
                            sout.inc(SY.dma_start(
                                out=xout[128 * (j - 1):128 * j, 0:512],
                                in_=x[j][:, 0:512]), 16)
                            sout.inc(SY.dma_start(
                                out=xout[128 * (j - 1):128 * j, 512:1024],
                                in_=x[j][:, 512:1024]), 16)
                while stats_pending:
                    stats_cb(stats_pending.pop(0))
                return ev_done

        def wait_none(V_, j):
            pass
        for e_ in (T, S, V, G):
            e_.wait_ge(swa.h, swa_done)
        st2_cp = {}

        def cb_stats2(j):
            st2_cp["cp"] = emit_stats_tile(j, j, nmu2)
        ffn1_ev = emit_ffn(list(range(NXT)), nmu1, rsd1, stats1_ready,
                           wait_none, False, stats_cb=cb_stats2)

        KFFN1 = bool(_os.environ.get("KFFN1"))
        if KFFN1:
            for k in range(NKD):
                sout.inc(SY.dma_start(out=d_w1tl8[k], in_=w1tl_stash[k][:]), 16)
            V.wait_ge(cv.h, stats1_ready)
            V.tensor_copy(out=rtmp[:, 0:17], in_=nmu1[:])
            gS = cv.inc(V.tensor_copy(out=rtmp[:, 17:32], in_=rsd1[:, 0:15]))
            SY.wait_ge(cv.h, gS)
            sout.inc(SY.dma_start(out=d_st[0:128, 0:32], in_=rtmp[:]), 16)
            for j in range(1, NXT):
                SY.wait_ge(cv.h, ffn1_ev[j])
                sout.inc(SY.dma_start(out=xout[128 * (j - 1):128 * j, :],
                                      in_=x[j][:]), 16)
        if not KFFN1:
            barrier([V, S, T, G, SY])
            if KDUMP:
                for j in range(NXT):
                    sout.inc(SY.dma_start(out=d_x1[128 * j:128 * (j + 1), :],
                                          in_=x[j][:]), 16)
            stats2_ready = emit_stats_fin(NXT, rsd2, st2_cp["cp"])
            V.tensor_scalar_add(a2t[:], rsd2[:], 1.0)
            g = cv.inc(V.tensor_mul(b2ct[:], nmu2[:], rsd2[:]))
            V.wait_ge(cv.h, g)
            zadd_ready = cv.v
            st3_cp = {}

            with ExitStack() as es:
                wi_sb = [es.enter_context(nc.sbuf_tensor(f"wi{k}", [128, DS], F32))
                         for k in range(NKD)]
                w1a_sb = [es.enter_context(nc.sbuf_tensor(f"w1a{k}", [128, DS], F32))
                          for k in range(NKD)]
                for k in range(NKD):
                    swa.inc(G.dma_start(out=wi_sb[k][:],
                                        in_=wi_d[128 * k:128 * (k + 1), :]), 16)
                    swa.inc(G.dma_start(out=w1a_sb[k][:],
                                        in_=w1a_d[128 * k:128 * (k + 1), :]), 16)
                T.wait_ge(swa.h, swa.v)
                hA = [es.enter_context(nc.sbuf_tensor(f"hA{i}", [128, TH], F32))
                      for i in range(2)]
                hB = [es.enter_context(nc.sbuf_tensor(f"hB{i}", [128, 2112], F32))
                      for i in range(2)]
                zbuf = es.enter_context(nc.sbuf_tensor("zbuf", [128, D], F32))
                zT0 = [es.enter_context(nc.sbuf_tensor(f"zT{i}", [128, 128], F32))
                       for i in range(NKD)]
                k_g = [es.enter_context(nc.sbuf_tensor(f"k_g{i}", [128, 576], F32))
                       for i in range(2)]
                v_g = [es.enter_context(nc.sbuf_tensor(f"v_g{i}", [64, 256], F32))
                       for i in range(9)]
                q_g = [es.enter_context(nc.sbuf_tensor(f"q_g{i}", [128, 512], F32))
                       for i in range(2)]
                mx_g = [es.enter_context(nc.sbuf_tensor(f"mx_g{i}", [128, 512], F32))
                        for i in range(2)]
                sm_g = [es.enter_context(nc.sbuf_tensor(f"sm_g{i}", [128, 512], F32))
                        for i in range(2)]
                th_g = [es.enter_context(nc.sbuf_tensor(f"th_g{i}", [128, 512], F32))
                        for i in range(2)]
                d1_g = [es.enter_context(nc.sbuf_tensor(f"d1_g{i}", [128, 512], F32))
                        for i in range(2)]
                sc_sb = [es.enter_context(nc.sbuf_tensor(f"sc_sb{i}", [128, 192], F32))
                         for i in range(2)]
                ef_sb = [es.enter_context(nc.sbuf_tensor(f"ef_sb{i}", [128, 192], F32))
                         for i in range(2)]
                mk_sb = [es.enter_context(nc.sbuf_tensor(f"mk_sb{i}", [128, 192], F32))
                         for i in range(2)]
                top8 = [es.enter_context(nc.sbuf_tensor(f"top8_{i}", [128, 8], F32))
                        for i in range(2)]
                negm = [es.enter_context(nc.sbuf_tensor(f"negm{i}", [128, 1], F32))
                        for i in range(2)]
                den = [es.enter_context(nc.sbuf_tensor(f"den{i}", [128, 1], F32))
                       for i in range(2)]
                rd_ = [es.enter_context(nc.sbuf_tensor(f"rd_{i}", [128, 1], F32))
                       for i in range(2)]
                wT64 = [[es.enter_context(nc.sbuf_tensor(f"wT64_{s}_{i}", [64, 128], F32))
                         for i in range(3)] for s in range(2)]
                ph_sb = [es.enter_context(nc.sbuf_tensor(f"ph_sb{i}", [128, 136], F32))
                         for i in range(2)]
                pb_sb = [es.enter_context(nc.sbuf_tensor(f"pb_sb{i}", [128, 136], F32))
                         for i in range(2)]
                gate_row = es.enter_context(
                    nc.sbuf_tensor("gate_row", [1, 2112], F32))

                zT = [[t[:] for t in zT0],
                      [th_g[c // 4][:, 128 * (c % 4):128 * (c % 4) + 128]
                       for c in range(NKD)]]
                h0_mark = {}

                def emit_h0_mm(jq):
                    T.wait_ge(cs.h, h0_mark[(jq, "ev")])
                    if jq >= 1:
                        T.wait_ge(cs.h, h0_mark[(jq - 1, "h")])
                    for m in range(2):
                        for kk in range(NKD):
                            last = T.matmul(p[m][0:128, 0:128],
                                            wi_sb[kk][:, 128 * m:128 * (m + 1)],
                                            zT[jq % 2][kk], start=(kk == 0),
                                            stop=(kk == NKD - 1))
                    mmv = tg(p[5][0:1, 0:1])
                    h0_mark[(jq, "mm")] = mmv
                    S.wait_ge(ct.h, mmv)
                    for m in range(2):
                        last = S.activation(out=hA[m][:, 128 * jq:128 * (jq + 1)],
                                            in_=p[m][0:128, 0:128],
                                            func=AF.Identity,
                                            bias=bias2["bi_t"][:, m:m + 1], scale=1.0)
                    h0_mark[(jq, "h")] = cs.inc(last)

                for j in range(NXT):
                    V.wait_ge(cv.h, stats2_ready)
                    if j >= 1:
                        V.wait_ge(ct.h, h0_mark[(j - 1, "tr")])
                    zv = cv.inc(V.tensor_scalar(
                        out=zbuf[:], in0=x[j][:], scalar1=nmu2[:, j:j + 1],
                        scalar2=rsd2[:, j:j + 1], op0=ALU.add, op1=ALU.mult))
                    T.wait_ge(cv.h, zv)
                    if j >= 1:
                        T.wait_ge(cs.h, h0_mark[(j - 1, "ev")])
                    for c in range(NKD):
                        bank = 6 + (c // 4)
                        sl = (c % 4) * 128
                        last = T.transpose(p[bank][0:128, sl:sl + 128],
                                           zbuf[:, 128 * c:128 * (c + 1)], ident[:])
                    h0_mark[(j, "tr")] = tg(p[5][0:1, 0:1])
                    S.wait_ge(ct.h, h0_mark[(j, "tr")])
                    if j >= 2:
                        S.wait_ge(ct.h, h0_mark[(j - 2, "mm")])
                    for c in range(NKD):
                        bank = 6 + (c // 4)
                        sl = (c % 4) * 128
                        last = S.copy(out=zT[j % 2][c],
                                      in_=p[bank][0:128, sl:sl + 128])
                    h0_mark[(j, "ev")] = cs.inc(last)
                    if j >= 1:
                        emit_h0_mm(j - 1)
                emit_h0_mm(NXT - 1)
                V.wait_ge(cs.h, h0_mark[(NXT - 1, "h")])
                V.tensor_single_scalar(hA[0][:, 0:128], hA[0][:, 0:128], hscal[:], ALU.mult)
                g = cv.inc(V.tensor_single_scalar(hA[1][:, 0:128], hA[1][:, 0:128],
                                                  hscal[:], ALU.mult))
                h0_ready = (cs.v, cv.v)

                def emit_attn_step(IN, OUT, Sn, blocks, blk0_ob, tok0_col, in_ready):
                    Tq = Sn - 64
                    nch = Sn // 16
                    T.wait_ge(cs.h, in_ready[0])
                    T.wait_ge(cv.h, in_ready[1])
                    V.wait_ge(cs.h, in_ready[0])
                    S.wait_ge(cv.h, in_ready[1])
                    for c in range(2):
                        x3 = IN[c][:, 0:Sn].rearrange("p (a b) -> p a b", b=16)
                        last = V.reduce_sum(ph_sb[c][:, 0:nch], x3, axis=AX.X)
                    g = cv.inc(last)
                    V.wait_ge(cv.h, g)
                    T.wait_ge(cv.h, g)
                    for m in range(2):
                        for kk in range(2):
                            last = T.matmul(p[4][0:128, 0:nch],
                                            wp_sb[kk][:, 128 * m:128 * (m + 1)],
                                            ph_sb[kk][:, 0:nch],
                                            start=(kk == 0), stop=(kk == 1))
                        tv = tg(p[5][0:1, 0:1])
                        S.wait_ge(ct.h, tv)
                        last = S.activation(out=pb_sb[m][:, 0:nch],
                                            in_=p[4][0:128, 0:nch], func=AF.Identity,
                                            bias=bias2["bp_t"][:, m:m + 1],
                                            scale=sc16[:, 0:1])
                        pbv = cs.inc(last)
                        T.wait_ge(cs.h, pbv)
                    pb_ready = cs.v
                    for t0 in range(0, Tq, 512):
                        n = min(512, Tq - t0)
                        for kk in range(2):
                            last = T.matmul(p[4][0:1, 0:n],
                                            wh_sb[kk][:, 2:3],
                                            IN[kk][:, 64 + t0:64 + t0 + n],
                                            start=(kk == 0), stop=(kk == 1))
                        tv = tg(p[5][0:1, 0:1])
                        S.wait_ge(ct.h, tv)
                        gv = cs.inc(S.activation(out=gate_row[0:1, t0:t0 + n],
                                                 in_=p[4][0:1, 0:n], func=AF.Sigmoid,
                                                 bias=bh_sb[0:1, 2:3], scale=1.0))
                        T.wait_ge(cs.h, gv)
                    gate_ready = cs.v

                    groups = [blocks[i:i + 4] for i in range(0, len(blocks), 4)]
                    out_marks = []
                    sm_ev_par = {}
                    sc_ev_par = {}
                    wt_ev_par = {}
                    prev_d1v = None
                    mx_ready = {}
                    mx_free = {}

                    def mx_ap(stg, c, n):
                        if stg == 0:
                            return mx_g[c][:, 0:n]
                        return zbuf[:, 512 * c:512 * c + n]

                    def emit_conv(g):
                        grp_g = groups[g]
                        gb0_g = grp_g[0][0]
                        gw_g = sum(q for _, q in grp_g)
                        stg = g % 2
                        if stg in mx_free:
                            V.wait_ge(ct.h, mx_free[stg])
                        for c in range(2):
                            base_g = 64 + gb0_g - 4
                            V.tensor_scalar(out=mx_ap(stg, c, gw_g),
                                            in0=IN[c][:, base_g:base_g + gw_g],
                                            scalar1=cw_sb[:, 5 * c:5 * c + 1],
                                            scalar2=bias2["cb_t"][:, c:c + 1],
                                            op0=ALU.mult, op1=ALU.add)
                            for kk in range(1, 5):
                                last_g = V.scalar_tensor_tensor(
                                    out=mx_ap(stg, c, gw_g),
                                    in0=IN[c][:, base_g + kk:base_g + kk + gw_g],
                                    scalar=cw_sb[:, 5 * c + kk:5 * c + kk + 1],
                                    in1=mx_ap(stg, c, gw_g),
                                    op0=ALU.mult, op1=ALU.add)
                        mx_ready[g] = cv.inc(last_g)

                    emit_conv(0)
                    for gi_c, grp in enumerate(groups):
                        gb0 = grp[0][0]
                        gw = sum(q for _, q in grp)
                        cw_ = gw + 64
                        qc = slice(64 + gb0, 64 + gb0 + gw)
                        if 1 in sc_ev_par:
                            T.wait_ge(cv.h, sc_ev_par[1])
                        pos = 0
                        k_prev_ev = None
                        while pos < cw_:
                            n = min(512, cw_ - pos)
                            if k_prev_ev is not None:
                                T.wait_ge(cs.h, k_prev_ev)
                            for m in range(2):
                                for kk in range(2):
                                    last = T.matmul(
                                        p[m][0:128, 0:n],
                                        wk_sb[kk][:, 128 * m:128 * (m + 1)],
                                        IN[kk][:, gb0 + pos:gb0 + pos + n],
                                        start=(kk == 0), stop=(kk == 1))
                                tv = tg(p[5][0:1, 0:1])
                                S.wait_ge(ct.h, tv)
                                ev = cs.inc(S.activation(
                                    out=k_g[m][:, pos:pos + n], in_=p[m][0:128, 0:n],
                                    func=AF.Identity,
                                    bias=bias2["bk_t"][:, m:m + 1], scale=1.0))
                            k_prev_ev = ev
                            pos += n
                        nv = cw_ // 64
                        if prev_d1v is not None:
                            T.wait_ge(cs.h, prev_d1v)
                        T.wait_ge(cs.h, k_prev_ev)
                        v_evs = {}
                        for t in range(nv):
                            bank = 2 + (t % 2)
                            if t >= 2:
                                T.wait_ge(cs.h, v_evs[t - 2])
                            for kk in range(2):
                                last = T.matmul(
                                    p[bank][0:64, 0:256],
                                    IN[kk][:, gb0 + 64 * t:gb0 + 64 * (t + 1)],
                                    wv_sb[kk][:], start=(kk == 0), stop=(kk == 1))
                            tv = tg(p[5][0:1, 0:1])
                            S.wait_ge(ct.h, tv)
                            v_evs[t] = cs.inc(S.copy(out=v_g[t][:],
                                                     in_=p[bank][0:64, 0:256]))
                        kv_ready = cs.v
                        T.wait_ge(cs.h, v_evs[nv - 1])
                        for m in range(2):
                            for kk in range(2):
                                last = T.matmul(
                                    p[m][0:128, 0:gw],
                                    wq_sb[kk][:, 128 * m:128 * (m + 1)],
                                    IN[kk][:, qc], start=(kk == 0), stop=(kk == 1))
                            tv = tg(p[5][0:1, 0:1])
                            S.wait_ge(ct.h, tv)
                            ev = cs.inc(S.activation(
                                out=q_g[m][:, 0:gw], in_=p[m][0:128, 0:gw],
                                func=AF.Identity, bias=bias2["bq_t"][:, m:m + 1],
                                scale=1.0))
                        T.wait_ge(cs.h, ev)
                        q_ready = cs.v
                        V.wait_ge(cs.h, q_ready)

                        scv_of = {}

                        def emit_scores(bi_):
                            ob, qlen = grp[bi_]
                            par = bi_ % 2
                            lc = ob - gb0
                            cl = qlen + 64
                            sb = p[4] if par == 0 else p[0]
                            if par in sc_ev_par:
                                T.wait_ge(cv.h, sc_ev_par[par])
                            for kk in range(2):
                                last = T.matmul(
                                    sb[0:qlen, 0:cl],
                                    q_g[kk][:, lc:lc + qlen],
                                    k_g[kk][:, lc:lc + cl],
                                    start=(kk == 0), stop=(kk == 1))
                            scv_of[bi_] = tg(p[5][0:1, 0:1])

                        emit_scores(0)
                        for bi_, (ob, qlen) in enumerate(grp):
                            par = bi_ % 2
                            lc = ob - gb0
                            cl = qlen + 64
                            mask = blk0m if ob == blk0_ob else bandm
                            sb = p[4] if par == 0 else p[0]
                            if bi_ + 1 < len(grp):
                                emit_scores(bi_ + 1)
                            V.wait_ge(ct.h, scv_of[bi_])
                            sc_ev_par[par] = cv.inc(V.scalar_tensor_tensor(
                                out=sc_sb[par][0:qlen, 0:cl],
                                in0=sb[0:qlen, 0:cl], scalar=1.0 / 16.0,
                                in1=mask[0:qlen, 0:cl], op0=ALU.mult, op1=ALU.add))
                            g8 = cv.inc(V.max(out=top8[par][0:qlen, :],
                                              in_=sc_sb[par][0:qlen, 0:cl]))
                            V.wait_ge(cv.h, g8)
                            nmv = cv.inc(V.tensor_scalar_mul(
                                negm[par][0:qlen, :], top8[par][0:qlen, 0:1], -1.0))
                            S.wait_ge(cv.h, nmv)
                            S.activation(out=ef_sb[par][0:qlen, 0:8],
                                         in_=top8[par][0:qlen, :], func=AF.Exp,
                                         bias=negm[par][0:qlen, :], scale=1.0,
                                         accum_out=den[par][0:qlen, :])
                            exv = cs.inc(S.activation(
                                out=ef_sb[par][0:qlen, 0:cl],
                                in_=sc_sb[par][0:qlen, 0:cl], func=AF.Exp,
                                bias=negm[par][0:qlen, :], scale=1.0))
                            V.wait_ge(cs.h, exv)
                            rdv = cv.inc(V.reciprocal(rd_[par][0:qlen, :],
                                                      den[par][0:qlen, :]))
                            V.wait_ge(cv.h, rdv)
                            V.tensor_single_scalar(mk_sb[par][0:qlen, 0:cl],
                                                   sc_sb[par][0:qlen, 0:cl],
                                                   top8[par][0:qlen, 7:8], ALU.is_ge)
                            wfv = cv.inc(V.scalar_tensor_tensor(
                                out=ef_sb[par][0:qlen, 0:cl],
                                in0=ef_sb[par][0:qlen, 0:cl],
                                scalar=rd_[par][0:qlen, :],
                                in1=mk_sb[par][0:qlen, 0:cl],
                                op0=ALU.mult, op1=ALU.mult))
                            T.wait_ge(cv.h, wfv)
                            if par in wt_ev_par:
                                T.wait_ge(cs.h, wt_ev_par[par])
                            npieces = cl // 64
                            for pi in range(npieces):
                                sl = pi * 128
                                last = T.transpose(
                                    p[6 + par][0:64, sl:sl + qlen],
                                    ef_sb[par][0:qlen, 64 * pi:64 * (pi + 1)],
                                    ident[0:qlen, 0:qlen])
                            trv = tg(p[5][0:1, 0:1])
                            S.wait_ge(ct.h, trv)
                            for pi in range(npieces):
                                sl = pi * 128
                                last = S.copy(out=wT64[par][pi][0:64, 0:qlen],
                                              in_=p[6 + par][0:64, sl:sl + qlen])
                            wev = cs.inc(last)
                            wt_ev_par[par] = wev
                            T.wait_ge(cs.h, wev)
                            for m in range(2):
                                off = m * 128
                                for pi in range(npieces):
                                    vt = lc // 64 + pi
                                    last = T.matmul(
                                        p[2 + par][0:128, off:off + qlen],
                                        v_g[vt][0:64, 128 * m:128 * (m + 1)],
                                        wT64[par][pi][0:64, 0:qlen],
                                        start=(pi == 0), stop=(pi == npieces - 1))
                            smv = tg(p[5][0:1, 0:1])
                            S.wait_ge(ct.h, smv)
                            for m in range(2):
                                off = m * 128
                                last = S.activation(
                                    out=sm_g[m][:, lc:lc + qlen],
                                    in_=p[2 + par][0:128, off:off + qlen],
                                    func=AF.Identity,
                                    bias=bias2["bv_t"][:, m:m + 1], scale=1.0)
                            smev = cs.inc(last)
                            sm_ev_par[par] = smev
                            if bi_ == 1 and gi_c + 1 < len(groups):
                                emit_conv(gi_c + 1)
                            if tok0_col is not None and ob == blk0_ob:
                                c0 = tok0_col - gb0
                                V.wait_ge(cs.h, smev)
                                V.tensor_single_scalar(sm_g[0][:, c0:c0 + 1],
                                                       sm_g[0][:, c0:c0 + 1],
                                                       t0scal[:], ALU.mult)
                                cv.inc(V.tensor_single_scalar(
                                    sm_g[1][:, c0:c0 + 1], sm_g[1][:, c0:c0 + 1],
                                    t0scal[:], ALU.mult))
                        summ_ready = (cs.v, cv.v)

                        T.wait_ge(cs.h, summ_ready[0])
                        T.wait_ge(cv.h, summ_ready[1])
                        T.wait_ge(cv.h, mx_ready[gi_c])
                        T.wait_ge(cs.h, pb_ready)
                        pbc0 = (64 + gb0) // 16
                        rhs_list = [IN[0][:, qc], IN[1][:, qc],
                                    mx_ap(gi_c % 2, 0, gw), mx_ap(gi_c % 2, 1, gw),
                                    pb_sb[0][:, pbc0:pbc0 + gw // 16]
                                    .unsqueeze(2).broadcast_to([128, gw // 16, 16]),
                                    pb_sb[1][:, pbc0:pbc0 + gw // 16]
                                    .unsqueeze(2).broadcast_to([128, gw // 16, 16]),
                                    sm_g[0][:, 0:gw], sm_g[1][:, 0:gw]]
                        for m in range(2):
                            for kk in range(NKD):
                                last = T.matmul(p[m][0:128, 0:gw],
                                                w1a_sb[kk][:, 128 * m:128 * (m + 1)],
                                                rhs_list[kk], start=(kk == 0),
                                                stop=(kk == NKD - 1))
                            uiv = tg(p[5][0:1, 0:1])
                            mx_free[gi_c % 2] = uiv
                            S.wait_ge(ct.h, uiv)
                            thv = cs.inc(S.activation(
                                out=th_g[m][:, 0:gw], in_=p[m][0:128, 0:gw],
                                func=AF.Tanh, bias=bias2["b1a_t"][:, m:m + 1],
                                scale=1.0))
                            T.wait_ge(cs.h, thv)
                        for m in range(2):
                            for kk in range(2):
                                last = T.matmul(p[2 + m][0:128, 0:gw],
                                                w2a_sb[kk][:, 128 * m:128 * (m + 1)],
                                                th_g[kk][:, 0:gw], start=(kk == 0),
                                                stop=(kk == 1))
                        dv = tg(p[5][0:1, 0:1])
                        T.wait_ge(cs.h, gate_ready)
                        gbv = ct.inc(T.matmul(p[6][0:128, 0:gw], ones_r[:],
                                              gate_row[0:1, gb0:gb0 + gw],
                                              start=True, stop=True))
                        S.wait_ge(ct.h, dv)
                        for m in range(2):
                            last = S.activation(out=d1_g[m][:, 0:gw],
                                                in_=p[2 + m][0:128, 0:gw],
                                                func=AF.Identity,
                                                bias=bias2["b2a_t"][:, m:m + 1],
                                                scale=1.0)
                        d1v = cs.inc(last)
                        prev_d1v = d1v
                        V.wait_ge(cs.h, d1v)
                        V.wait_ge(ct.h, gbv)
                        for m in range(2):
                            V.tensor_mul(d1_g[m][:, 0:gw], d1_g[m][:, 0:gw],
                                         p[6][0:128, 0:gw])
                            last = V.tensor_add(OUT[m][:, gb0:gb0 + gw],
                                                d1_g[m][:, 0:gw], IN[m][:, qc])
                        out_marks.append(cv.inc(last))
                    return out_marks[-1]

                if KDUMP:
                    SY.wait_ge(cs.h, h0_ready[0])
                    SY.wait_ge(cv.h, h0_ready[1])
                    for m in range(2):
                        sout.inc(SY.dma_start(out=d_h0[m, :, :], in_=hA[m][:]), 16)
                blocks1 = [(0, 64)] + [(64 + 128 * b, 128) for b in range(16)]
                s1v = emit_attn_step(hA, hB, TH, blocks1, 64, 64, h0_ready)
                V.wait_ge(cv.h, s1v)
                V.tensor_single_scalar(hB[0][:, 0:64], hB[0][:, 0:64], hscal[:], ALU.mult)
                g = cv.inc(V.tensor_single_scalar(hB[1][:, 0:64], hB[1][:, 0:64],
                                                  hscal[:], ALU.mult))
                h1_ready = (cs.v, cv.v)

                if KDUMP:
                    SY.wait_ge(cs.h, h1_ready[0])
                    SY.wait_ge(cv.h, h1_ready[1])
                    for m in range(2):
                        sout.inc(SY.dma_start(out=d_h1[m, :, :], in_=hB[m][:]), 16)
                blocks2 = [(128 * b, 128) for b in range(16)]
                s2v = emit_attn_step(hB, hA, 2112, blocks2, 0, 0, h1_ready)

                if KDUMP:
                    SY.wait_ge(cv.h, s2v)
                    for m in range(2):
                        sout.inc(SY.dma_start(out=d_h2[m, :, :],
                                              in_=hA[m][:, 0:TOWN]), 16)
                x2_ev = {}
                for j in range(1, NXT):
                    hc = slice(128 * (j - 1), 128 * j)
                    bk = 2 * ((j - 1) % 2)
                    T.wait_ge(cv.h, s2v)
                    if j >= 3:
                        T.wait_ge(cv.h, x2_ev[j - 2])
                    for half in range(2):
                        for kk in range(2):
                            last = T.matmul(p[bk + half][0:128, 0:512],
                                            hA[kk][:, hc],
                                            wo_sb[kk][:, 512 * half:512 * (half + 1)],
                                            start=(kk == 0), stop=False)
                        last = T.matmul(p[bk + half][0:128, 0:512], ones_r[0:1, 0:128],
                                        bo_sb[0:1, 512 * half:512 * (half + 1)],
                                        start=False, stop=True)
                    uv = tg(p[5][0:1, 0:1])
                    V.wait_ge(ct.h, uv)
                    V.wait_ge(cv.h, zadd_ready)
                    V.tensor_scalar(out=zbuf[:], in0=x[j][:],
                                    scalar1=a2t[:, j:j + 1], scalar2=b2ct[:, j:j + 1],
                                    op0=ALU.mult, op1=ALU.add)
                    for half in range(2):
                        hcd = slice(512 * half, 512 * (half + 1))
                        last = V.scalar_tensor_tensor(
                            out=x[j][:, hcd], in0=p[bk + half][0:128, 0:512],
                            scalar=1.0,
                            in1=zbuf[:, hcd], op0=ALU.mult, op1=ALU.add)
                    x2_ev[j] = cv.inc(last)
                    st3_cp["cp"] = emit_stats_tile(j - 1, j, nmu3)

            if KDUMP:
                for j in range(1, NXT):
                    SY.wait_ge(cv.h, x2_ev[j])
                    sout.inc(SY.dma_start(out=d_x2[128 * j:128 * (j + 1), :],
                                          in_=x[j][:]), 16)
            barrier([V, S, T, G, SY])
            stats3_ready = emit_stats_fin(NXT - 1, rsd3, st3_cp["cp"])
            emit_ffn(list(range(1, NXT)), nmu3, rsd3, stats3_ready,
                     wait_none, True)

        SY.wait_ge(sout.h, sout.v)

    nc.finalize()
    return nc


_PROG = None


def _get_prog():
    global _PROG
    if _PROG is None:
        _PROG = build_program()
    return _PROG


def kernel(**inputs):
    f32 = np.float32
    P = {k: np.ascontiguousarray(np.asarray(v, f32)) for k, v in inputs.items()}
    x_full = P["L0_attn_x"].reshape(2 * 8192, D)

    def t2(b):
        return np.ascontiguousarray(b.reshape(2, 128).T)

    cw = P["L0_attn_conv_w"]
    cw_t = np.ascontiguousarray(
        np.transpose(cw.reshape(5, 2, 128), (2, 1, 0)).reshape(128, 10))
    common = {
        "wi": P["L0_attn_Wi"], "wq": P["L0_attn_Wq"], "wk": P["L0_attn_Wk"],
        "wv": P["L0_attn_Wv"], "wp": P["L0_attn_Wp"], "w2a": P["L0_attn_W2"],
        "w1a": P["L0_attn_W1"], "wh": P["L0_attn_Wh"], "wo": P["L0_attn_Wo"],
        "w1f": np.ascontiguousarray(P["L0_ffn_W1"].T),
        "w2f": np.ascontiguousarray(P["L0_ffn_W2"].T),
        "bi_t": t2(P["L0_attn_bi"]), "bq_t": t2(P["L0_attn_bq"]),
        "bk_t": t2(P["L0_attn_bk"]), "bv_t": t2(P["L0_attn_bv"]),
        "bp_t": t2(P["L0_attn_bp"]), "b2a_t": t2(P["L0_attn_b2"]),
        "b1a_t": t2(P["L0_attn_b1"]), "cb_t": t2(P["L0_attn_conv_b"]),
        "b1f_t": np.ascontiguousarray(P["L0_ffn_b1"].reshape(32, 128).T),
        "cw_t": cw_t,
        "bh_r": P["L0_attn_bh"].reshape(1, 3),
        "bo_r": P["L0_attn_bo"].reshape(1, D),
        "b2f_r": P["L0_ffn_b2"].reshape(1, D),
        "ident": np.eye(128, dtype=f32),
    }
    r = np.arange(128)[:, None]
    c = np.arange(192)[None, :]
    band = np.where((c >= r) & (c < r + 64), 0.0, NEG).astype(f32)
    blk0 = np.where((c >= r) & (c < r + 64) & (c >= 64), 0.0, NEG).astype(f32)

    in_maps = []
    for core in range(NCORE):
        start = TOWN * core
        first = core % 4 == 0
        xin = np.zeros((TH, D), f32)
        xin[HALO:] = x_full[start:start + TOWN]
        if not first:
            xin[0:HALO] = x_full[start - HALO:start]
        m = dict(common)
        m["xin"] = xin
        m["bandm"] = band
        m["blk0m"] = blk0 if first else band
        m["halo_scal"] = np.full((128, 1), 0.0 if first else 1.0, f32)
        m["tok0_scal"] = np.full((128, 1), 0.0 if first else 1.0, f32)
        in_maps.append(m)

    nc = _get_prog()
    res = run_bass_kernel_spmd(nc, in_maps, core_ids=list(range(NCORE)))
    out = np.concatenate([np.asarray(res.results[cc]["xout"])
                          for cc in range(NCORE)], axis=0)
    return out.reshape(2, 8192, D)
